# version 45
# baseline (speedup 1.0000x reference)
import numpy as np
import ml_dtypes
from contextlib import ExitStack

import concourse.bass as bass
import concourse.mybir as mybir
from concourse.bass_utils import run_bass_kernel_spmd

F32 = mybir.dt.float32
BF16 = mybir.dt.bfloat16
U8 = mybir.dt.uint8
AF = mybir.ActivationFunctionType
ALU = mybir.AluOpType
AX = mybir.AxisListType

ENGINES = ("tensor", "vector", "scalar", "gpsimd", "sync")
SCALE_Q = float(64.0 ** (-0.25))
EPS_LN = 1e-5
EPS_ATTN = 1e-6
NCORES = 8
T = 2048
D = 1024
HID = 4096


class B:

    def __init__(self):
        import os
        self.nc = bass.Bass("TRN2", target_bir_lowering=False,
                            detect_race_conditions=bool(os.environ.get("RACEDET")))
        self.ctx = ExitStack()
        self.streams = {e: [] for e in ENGINES}
        self.cnt = {}
        self.semh = {}
        self.dma_sems = [f"q{qn}{i}" for qn in "sgav" for i in range(8)]
        for s in ("pe", "dv", "ac", "gp", "sy", *self.dma_sems):
            self.semh[s] = self.ctx.enter_context(self.nc.semaphore(s))
            self.cnt[s] = 0
        self.dmacnt = {"qs": 0, "qg": 0, "qa": 0, "qv": 0}
        self.tcount = 0
        self.phase_marks = []
        self.eng_sem = dict(tensor="pe", vector="dv", scalar="ac",
                            gpsimd="gp", sync="sy")

    _sbn = 0

    def sb(self, name, shape, dtype, stack=None):
        if name is None:
            B._sbn += 1
            name = f"sb{B._sbn}"
        return (stack or self.ctx).enter_context(
            self.nc.sbuf_tensor(name, shape, dtype))

    def ps(self, name, shape, dtype=F32):
        return self.ctx.enter_context(self.nc.psum_tensor(name, shape, dtype))

    def op(self, eng, fn, reads=(), inc=None, amt=1):
        if inc is None and eng in ("vector", "scalar", "gpsimd"):
            inc = self.eng_sem[eng]
        st = self.streams[eng]
        flat = []
        for tk in reads:
            if tk is None:
                continue
            if isinstance(tk, tuple) and len(tk) == 3 and isinstance(tk[0], str):
                flat.append(tk)
            else:
                flat.extend(t for t in tk if t is not None)
        for tk in flat:
            s, v, peng = tk
            if peng == eng == "tensor":
                continue
            assert v <= self.cnt[s], f"future token {s}={v} > {self.cnt[s]}"
            st.append(("w", s, v))
        if eng in ("vector", "scalar", "gpsimd"):
            own = self.eng_sem[eng]
            if self.cnt[own]:
                st.append(("w", own, self.cnt[own]))
        if eng == "tensor":
            self.tcount += 1
        tok = None
        if inc is not None:
            self.cnt[inc] += amt
            tok = (inc, self.cnt[inc], eng)
        st.append(("i", fn, inc, amt if inc else 0))
        return tok

    def dma(self, eng, out, in_, reads=(), transpose=False):
        assert eng in ("sync", "gpsimd", "scalar", "vector")
        q = {"sync": "qs", "gpsimd": "qg", "scalar": "qa", "vector": "qv"}[eng]
        n = self.dmacnt[q]
        self.dmacnt[q] += 1
        sem = q[1] and f"q{q[1]}{n % 8}"
        rounds = n // 8
        reads = tuple(reads)
        if rounds > 0:
            reads = reads + ((sem, 16 * rounds, "_"),)
        return self.op(eng, lambda E, o=out, i=in_, tr=transpose:
                       E.dma_start(out=o, in_=i, transpose=tr),
                       reads, sem, 16)

    _pe_base = (0,)

    def phase(self, name):
        self.phase_marks.append((name, self.tcount))

    def pe_guard(self, *cfg):
        if cfg != self._pe_base:
            if self.cnt["pe"]:
                self.streams["tensor"].append(("w", "pe", self.cnt["pe"]))
            self._pe_base = cfg

    def fence(self):
        toks = [(se, self.cnt[se], "_") for se in ("pe", "dv", "ac", "gp")
                if self.cnt[se]]
        for ds in self.dma_sems:
            if self.cnt[ds]:
                toks.append((ds, self.cnt[ds], "_"))
        for e in ENGINES:
            own = self.eng_sem[e]
            for tk in toks:
                if tk[0] == own:
                    continue
                self.streams[e].append(("w", tk[0], tk[1]))

    def replay(self):
        nc = self.nc
        with nc.Block() as blk:
            def mk(eng):
                def body(E):
                    seen = {}
                    nw = 0
                    for item in self.streams[eng]:
                        if item[0] == "w":
                            _, s, v = item
                            if seen.get(s, 0) >= v:
                                continue
                            seen[s] = v
                            if nw >= 2:
                                E.nop(nofuse=True)
                                nw = 0
                            E.wait_ge(self.semh[s], v)
                            nw += 1
                        else:
                            nw = 0
                            _, fn, inc, amt = item
                            inst = fn(E)
                            if inc is not None and amt:
                                inst.then_inc(self.semh[inc], amt)
                return body
            blk.tensor(mk("tensor"))
            blk.vector(mk("vector"))
            blk.scalar(mk("scalar"))
            blk.gpsimd(mk("gpsimd"))
            blk.sync(mk("sync"))


def chunk_ap(dram_ap, gc):
    return dram_ap.rearrange("(n p) d -> n p d", p=128)[gc]


def build_program(debug_phases=("ffn1", "attn", "ffn2")):
    b = B()
    nc = b.nc

    x_in = nc.dram_tensor("x_in", [T, D], F32, kind="ExternalInput").ap()
    k0_d = nc.dram_tensor("k0", [2 * D, HID], U8, kind="ExternalInput").ap()
    k1_d = nc.dram_tensor("k1", [HID, HID], U8, kind="ExternalInput").ap()
    k2_d = nc.dram_tensor("k2", [HID, D], U8, kind="ExternalInput").ap()
    w_d = {w: nc.dram_tensor("W" + w, [D, D], F32, kind="ExternalInput").ap()
           for w in ("q", "k", "v", "o")}
    bo_d = nc.dram_tensor("bo", [1, D], F32, kind="ExternalInput").ap()
    sv_d = nc.dram_tensor("sv", [128, 3], F32, kind="ExternalInput").ap()
    id_d = nc.dram_tensor("ident", [128, 128], BF16, kind="ExternalInput").ap()
    out_d = nc.dram_tensor("out", [T, D], F32, kind="ExternalOutput").ap()

    x_dram = nc.dram_tensor("x_scr", [T, D], F32, kind="Internal").ap()
    d0_dram = nc.dram_tensor("d0_scr", [D, HID], BF16, kind="Internal").ap()
    k1_pm = nc.dram_tensor("k1pm_scr", [HID, HID], BF16, kind="Internal").ap()
    k2_pm = nc.dram_tensor("k2pm_scr", [HID, D], BF16, kind="Internal").ap()
    wbf_dram = nc.dram_tensor("wbf_scr", [4 * D, D], BF16, kind="Internal").ap()

    idn = b.sb("idn", [128, 128], BF16)
    scol = b.sb("scol", [128, 3], F32)
    deriv = b.sb("deriv", [128, 3], F32)
    onescol = b.sb("onescol", [128, 1], BF16)
    onesrow = b.sb("onesrow", [1, 128], BF16)
    bo_b = b.sb("bo_b", [1, D], BF16)
    xst = b.sb("xst", [128, 2, D], F32)
    qb = b.sb("qb", [128, 2, D], BF16)
    stat = b.sb("stat", [128, 8, 16], F32)
    rcpt = b.sb("rcpt", [128, 31], F32)
    epsc = b.sb("epsc", [128, 1], F32)
    neg2 = b.sb("neg2", [128, 1], F32)

    ps_a = b.ps("ps_a", [128, 3, 512])
    ps_b = b.ps("ps_b", [128, 2, 1024])
    ps_t = b.ps("ps_t", [128, 8, 128], BF16)
    pbt = [ps_b[:, n // 2, (n % 2) * 512:(n % 2 + 1) * 512] for n in range(4)]
    tile7 = pbt + [ps_a[:, i, :] for i in range(3)]

    def t7_tok(i):
        return st_["pb"][i] if i < 4 else st_["pa"][i - 4]

    def t7_set(i, tok):
        if i < 4:
            st_["pb"][i] = tok
        else:
            st_["pa"][i - 4] = tok

    st_ = {
        "pa": [None, None, None],
        "pb": [None, None, None, None],
        "pt": None,
        "xst": [None, None],
        "qb": [None, None, None],
        "qfm_gate": None,
        "scr1": None, "scr2": None,
        "mv": [None, None],
        "q_pend": [],
        "wstq": [None, None, None, None],
        "rot7": 0,
        "x_load_extra": [None] * 16,
        "slot_stash": [None, None, None, None],
        "built": False,
        "t_w_stash": None,
        "prev_l2_pe": None,
        "prev_store": None,
        "prev_xh_store": None,
    }

    t_id = b.dma("sync", idn[:], id_d)
    t_bo = b.dma("gpsimd", bo_b[:], bo_d)
    t_oc = b.op("vector", lambda E: E.memset(onescol[:], 1.0), (), "dv")
    t_or = b.op("vector", lambda E: E.memset(onesrow[:], 1.0), (), "dv")
    t_ep = b.op("vector", lambda E: E.memset(epsc[:], EPS_LN), (), "dv")
    t_n2 = b.op("vector", lambda E: E.memset(neg2[:], -2.0), (), "dv")
    t_scol = b.dma("gpsimd", scol[:, :], sv_d)
    t_d0c = b.op("vector", lambda E: E.tensor_scalar(
        out=deriv[:, 0:1], in0=scol[:, 0:1], scalar1=4.0, scalar2=None,
        op0=ALU.mult), (t_scol,), "dv")
    t_d1c = b.op("vector", lambda E: E.tensor_scalar(
        out=deriv[:, 1:2], in0=scol[:, 1:2], scalar1=2.0, scalar2=None,
        op0=ALU.mult), (t_scol,), "dv")
    t_d2c = b.op("vector", lambda E: E.tensor_scalar(
        out=deriv[:, 2:3], in0=scol[:, 2:3], scalar1=0.5, scalar2=None,
        op0=ALU.mult), (t_scol,), "dv")

    t_wstash = {}

    def wprep():
        for wi, wname in enumerate(("q", "k", "v", "o")):
            t_wstash[wname] = b.dma(
                "gpsimd", wbf_dram[wi * D:(wi + 1) * D, :], w_d[wname],
                reads=())


    def ln_load(src_ap, gc, page, pg_tok, q):
        return b.dma(q, page, chunk_ap(src_ap, gc),
                     reads=(pg_tok, st_["x_load_extra"][gc]))

    def ln_stats(src_ap, gc, c, page=None, tx_pre=None):
        sl = gc % 2
        ch = chunk_ap(src_ap, gc)
        if page is None:
            pg, is_xst = xst[:, sl, :], True
            gates = (st_["xst"][sl], st_["x_load_extra"][gc])
            tx_a = b.dma("sync", pg[:, 0:512], ch[:, 0:512], reads=gates)
            tx_b = b.dma("gpsimd", pg[:, 512:1024], ch[:, 512:1024],
                         reads=gates)
            tx = (tx_a, tx_b)
            r1, r2 = (tx_a,), (tx_b,)
        else:
            pg, is_xst = page, False
            tx = tx_pre
            r1 = r2 = (tx,)
        b1 = b.op("vector", lambda E, p=pg: E.bn_stats(
            out=stat[:, 0, 0:6], in_=p[:, 0:512]), r1)
        b2 = b.op("vector", lambda E, p=pg: E.bn_stats(
            out=stat[:, 0, 6:12], in_=p[:, 512:1024]), r2)
        mvsl = c % 2
        mv = stat[:, 1, 2 * mvsl:2 * mvsl + 2]
        ag = b.op("vector", lambda E, m=mv: E.bn_aggr(
            out=m, in_=stat[:, 0, 0:12]), (b1, b2, st_["mv"][mvsl]))
        tln = b.op("scalar", lambda E, m=mv, cc=c: E.activation(
            out=stat[:, 7, cc:cc + 1], in_=m[:, 1:2],
            func=AF.Ln, bias=epsc[:, 0:1]), (ag, t_ep))
        trs = b.op("scalar", lambda E, cc=c: E.activation(
            out=stat[:, 3, cc:cc + 1], in_=stat[:, 7, cc:cc + 1],
            func=AF.Exp, scale=-0.5), (tln,))
        return (sl, c, mv, trs, tx, pg, is_xst)

    def ln_apply(sl, c, mv, trs, tx, pg, is_xst, dst_fn, extra_xn_reads,
                 eng="vector"):
        dst = dst_fn(sl)
        txn = b.op(eng, lambda E, p=pg, cc=c, d=dst, m=mv: E.tensor_scalar(
            out=d, in0=p,
            scalar1=m[:, 0:1], scalar2=stat[:, 3, cc:cc + 1],
            op0=ALU.subtract, op1=ALU.mult),
            (trs, tx) + tuple(extra_xn_reads))
        st_["mv"][c % 2] = (txn,)
        if is_xst:
            st_["xst"][sl] = (txn,)
        return txn, sl

    def transpose8(src_getter, dst_ap, reads, eng="vector"):
        ttr = None
        for c2 in range(8):
            ttr = b.op("tensor", lambda E, c3=c2: E.transpose(
                ps_t[:, c3, :], src_getter(c3), idn[:]),
                tuple(reads) + (st_["pt"], t_id), "pe")
        if eng == "scalar":
            tev = b.op("scalar", lambda E, d=dst_ap: E.activation(
                out=d, in_=ps_t[:, :, :], func=AF.Copy), (ttr,), "ac")
        else:
            tev = b.op("vector", lambda E, d=dst_ap: E.tensor_copy(
                out=d, in_=ps_t[:, :, :]), (ttr,), "dv")
        st_["pt"] = tev
        return tev

    def ffn_phase(src_ap, dst_ap, fstack, l2_hook=None):
        b.pe_guard(0, 128, 0)
        a1 = b.sb(None, [128, 32, 1024], BF16, fstack)
        a2 = b.sb(None, [128, 32, 1024], BF16, fstack)
        wst = b.sb(None, [128, 4, 8, 512], BF16, fstack)
        u8a = b.sb(None, [128, 8, 512], U8, fstack)
        u8b = b.sb(None, [128, 8, 512], U8, fstack)
        xnf = b.sb(None, [128, 8, 1024], BF16, fstack)
        xh = a1[:, 0:16, :].bitcast(F32).rearrange("p (a c) n -> p a (c n)", c=2)
        a2x = a2[:, 0:16, :].bitcast(F32).rearrange("p (a c) n -> p a (c n)", c=2)

        u8_tok = [None, None]
        a1_evac = None
        a2_evac = None
        l1_done = None
        build = not st_["built"]
        l0_pe = [None, None]

        def wslot_load(qslot, view, u8_src, pm_blk, pm_src, first, sc_col):
            if first:
                tq = b.dma("gpsimd", view, u8_src,
                           reads=(st_["wstq"][qslot], st_["slot_stash"][qslot]))
                tcv = b.op("vector", lambda E, v=view: E.tensor_scalar(
                    out=v, in0=v, scalar1=2.0, scalar2=-1.0,
                    op0=ALU.mult, op1=ALU.add), (tq,), "dv")
                tst = b.dma("sync", pm_blk, view, reads=(tcv,))
                st_["t_w_stash"] = tst
                st_["slot_stash"][qslot] = tst
                return (tcv,), tst
            else:
                tq = b.dma("gpsimd", view, pm_src,
                           reads=(st_["wstq"][qslot], st_["slot_stash"][qslot],
                                  st_["t_w_stash"]))
                return (tq,), None

        def ln_half_steps(half):
            toks = []
            state = {"pend": None, "c": 0}

            def flush():
                sl, c, mv, trs, tx, pg, isx = state["pend"]
                txn, _ = ln_apply(sl, c, mv, trs, tx, pg, isx,
                                  lambda s: qb[:, s, :], (st_["qb"][sl],))
                st_["qb"][sl] = txn
                t_last = transpose8(
                    lambda c3, s=sl: qb[:, s, c3 * 128:(c3 + 1) * 128],
                    xnf[:, :, c * 128:(c + 1) * 128],
                    (txn, l0_pe[1 - half]))
                st_["qb"][sl] = t_last
                toks.append(t_last)
                state["pend"] = None

            def step():
                c = state["c"]
                if c < 8:
                    if half == 0:
                        if c == 0:
                            state["txs"] = [
                                ln_load(src_ap, cc, a2x[:, cc, :], None,
                                        ("sync", "scalar", "gpsimd")[cc % 3])
                                for cc in range(8)]
                        pend_new = ln_stats(src_ap, c, c, page=a2x[:, c, :],
                                            tx_pre=state["txs"][c])
                    else:
                        pend_new = ln_stats(src_ap, 8 + c, c)
                    if state["pend"] is not None:
                        flush()
                    state["pend"] = pend_new
                elif state["pend"] is not None:
                    flush()
                state["c"] = c + 1
            return step, toks

        def ln_half(half):
            step, toks = ln_half_steps(half)
            for _ in range(9):
                step()
            return toks

        def l0_half(half, t_xn_last):
            rp = 0
            nonlocal a1_evac
            for nb in range(8):
                sl = nb % 2
                dsl = wst[:, sl, 0:8, :]
                d0blk = d0_dram[:, nb * 512:(nb + 1) * 512].rearrange(
                    "(cc p) n -> p cc n", p=128)
                if build and half == 0:
                    ta = b.dma("sync", u8a[:],
                               k0_d[0:D, nb * 512:(nb + 1) * 512]
                               .rearrange("(cc p) n -> p cc n", p=128),
                               reads=(u8_tok[0],))
                    tb = b.dma("sync", u8b[:],
                               k0_d[D:2 * D, nb * 512:(nb + 1) * 512]
                               .rearrange("(cc p) n -> p cc n", p=128),
                               reads=(u8_tok[1],))
                    td1 = b.op("vector", lambda E, ds=dsl: E.tensor_tensor(
                        out=ds, in0=u8a[:], in1=u8b[:], op=ALU.subtract),
                        (ta, tb, st_["wstq"][sl], st_["slot_stash"][sl], t_d0c),
                        "dv")
                    u8_tok[0] = td1
                    u8_tok[1] = td1
                    tslot = b.op("vector", lambda E, ds=dsl: E.tensor_scalar(
                        out=ds, in0=ds, scalar1=deriv[:, 0:1], scalar2=None,
                        op0=ALU.mult), (), "dv")
                    st_["t_w_stash"] = b.dma("sync", d0blk, dsl, reads=(tslot,))
                    st_["slot_stash"][sl] = st_["t_w_stash"]
                    tslot = (tslot, st_["t_w_stash"])
                else:
                    tslot = b.dma("sync", dsl, d0blk,
                                  reads=(st_["wstq"][sl], st_["slot_stash"][sl],
                                         st_["t_w_stash"]))
                    tslot = (tslot,)
                grp_last = None
                for n in range(4):
                    for th in range(2):
                        rp = st_["rot7"]
                        st_["rot7"] = (rp + 1) % 7
                        reads = tslot + (t_xn_last[th * 4 + 3], t7_tok(rp))
                        mm = None
                        for cch in range(8):
                            mm = b.op("tensor",
                                      lambda E, r=rp, nn=n, t2=th, c3=cch, ds=dsl:
                                      E.matmul(tile7[r],
                                               ds[:, c3, nn * 128:(nn + 1) * 128],
                                               xnf[:, c3, t2 * 512:(t2 + 1) * 512],
                                               start=(c3 == 0), stop=(c3 == 7)),
                                      reads if cch == 0 else (),
                                      "pe" if cch == 7 else None)
                        grp_last = mm
                        ev_reads = [mm]
                        if nb == 0 and n == 0 and th == 0:
                            ev_reads.append(st_["prev_xh_store"])
                        tev = b.op("scalar", lambda E, r=rp, nn=n, t2=th, nb2=nb:
                                   E.activation(out=a1[:, nb2 * 4 + nn,
                                                       t2 * 512:(t2 + 1) * 512],
                                                in_=tile7[r], func=AF.Relu),
                                   ev_reads, "ac")
                        t7_set(rp, tev)
                        a1_evac = tev
                st_["wstq"][sl] = grp_last
            l0_pe[half] = grp_last

        b.phase(f'ffn-ln0')
        t_xn = ln_half(0)
        b.phase(f'ffn-l0a')
        l0_half(0, t_xn)
        b.phase('ffn-ln1')
        ln1_step, t_xn_next = ln_half_steps(1)
        for half in range(2):
            if half == 1:
                for _ in range(9):
                    ln1_step()
                l0_half(1, t_xn_next)
            b.phase(f'ffn-l1-{half}')
            if half == 0 and build and st_.get("wprep_fn") is not None:
                st_["wprep_fn"]()
                st_["wprep_fn"] = None
            for nb in range(8):
                if half == 0:
                    ln1_step()
                qtok = [None] * 4
                for q in range(4):
                    u8src = k1_d[q * 1024:(q + 1) * 1024,
                                 nb * 512:(nb + 1) * 512].rearrange(
                        "(cc p) n -> p cc n", p=128)
                    pmblk = k1_pm[q * 1024:(q + 1) * 1024,
                                  nb * 512:(nb + 1) * 512].rearrange(
                        "(cc p) n -> p cc n", p=128)
                    qtok[q], _ = wslot_load(q, wst[:, q, :, :], u8src,
                                            pmblk, pmblk,
                                            build and half == 0, None)
                for th in range(2):
                    tl = [0, 0, 0, 0]
                    mm_last = [None] * 4
                    for n in range(4):
                        tl[n] = st_["rot7"]
                        st_["rot7"] = (st_["rot7"] + 1) % 7
                    for c in range(32):
                        q = c // 8
                        for n in range(4):
                            qmark = (th == 1 and c % 8 == 7 and n == 3)
                            reads = []
                            if n == 0 and c % 8 == 0:
                                reads.append(qtok[q])
                            if c == 0:
                                reads += [t7_tok(tl[n]), a1_evac]
                            mm = b.op("tensor",
                                      lambda E, nn=n, c3=c, t2=th, qq=q, r=tl[n]:
                                      E.matmul(tile7[r],
                                               wst[:, qq, c3 % 8, nn * 128:(nn + 1) * 128],
                                               a1[:, c3, t2 * 512:(t2 + 1) * 512],
                                               start=(c3 == 0), stop=(c3 == 31)),
                                      reads, "pe" if (c == 31 or qmark) else None)
                            if c == 31:
                                mm_last[n] = mm
                            if qmark:
                                st_["wstq"][q] = mm
                    for n in range(4):
                        tev = b.op("scalar", lambda E, nn=n, t2=th, nb2=nb, r=tl[n]:
                                   E.activation(out=a2[:, nb2 * 4 + nn,
                                                       t2 * 512:(t2 + 1) * 512],
                                                in_=tile7[r], func=AF.Relu,
                                                scale=deriv[:, 1:2]),
                                   (mm_last[n], t_d1c), "ac")
                        t7_set(tl[n], tev)
                        a2_evac = tev
            l1_done = st_["wstq"][3]
            if half == 0:
                ln1_step()

            b.phase(f'ffn-l2-{half}')
            xh_tok = [None] * 8
            for t128 in range(8):
                gc = half * 8 + t128
                xh_tok[t128] = b.dma("sync", xh[:, t128, :], chunk_ap(src_ap, gc),
                                     reads=(l1_done, st_["prev_xh_store"],
                                            st_["prev_store"]))
            if half == 1 and l2_hook is not None:
                for gc in range(8):
                    l2_hook(gc, None, None)
            evac_tok = [None] * 8
            h_stores = []
            for nb in range(2):
                qtok2 = [None] * 4
                for q in range(4):
                    u8src = k2_d[q * 1024:(q + 1) * 1024,
                                 nb * 512:(nb + 1) * 512].rearrange(
                        "(cc p) n -> p cc n", p=128)
                    pmblk = k2_pm[q * 1024:(q + 1) * 1024,
                                  nb * 512:(nb + 1) * 512].rearrange(
                        "(cc p) n -> p cc n", p=128)
                    qtok2[q], _ = wslot_load(q, wst[:, q, :, :], u8src,
                                             pmblk, pmblk,
                                             build and half == 0, None)
                for t128 in range(8):
                    r3 = st_["rot7"]
                    st_["rot7"] = (r3 + 1) % 7
                    mm = None
                    for c in range(32):
                        q = c // 8
                        reads = []
                        if c % 8 == 0:
                            reads.append(qtok2[q])
                        if c == 0:
                            reads += [a2_evac, t7_tok(r3)]
                        qmark2 = (c % 8 == 7 and t128 == 7)
                        mm = b.op("tensor",
                                  lambda E, c3=c, tt=t128, qq=q, r=r3:
                                  E.matmul(tile7[r],
                                           a2[:, c3, tt * 128:(tt + 1) * 128],
                                           wst[:, qq, c3 % 8, :],
                                           start=(c3 == 0), stop=(c3 == 31)),
                                  reads, "pe" if (c == 31 or qmark2) else None)
                        if qmark2:
                            st_["wstq"][q] = mm
                    tst = b.op("vector", lambda E, tt=t128, r=r3, nb2=nb:
                               E.scalar_tensor_tensor(
                                   out=xh[:, tt, nb2 * 512:(nb2 + 1) * 512],
                                   in0=tile7[r], scalar=deriv[:, 2:3],
                                   in1=xh[:, tt, nb2 * 512:(nb2 + 1) * 512],
                                   op0=ALU.mult, op1=ALU.add),
                               (mm, t_d2c, xh_tok[t128]), "dv")
                    t7_set(r3, tst)
                    if nb == 1:
                        tstore = b.dma("sync", chunk_ap(dst_ap, half * 8 + t128),
                                       xh[:, t128, :], reads=(tst,))
                        h_stores.append(tstore)
                        st_["prev_store"] = tstore
                        st_["x_load_extra"][half * 8 + t128] = tstore
                        if half == 1 and l2_hook is not None:
                            l2_hook(8 + t128, xh[:, t128, :], (tst,))
            st_["prev_l2_pe"] = mm
            st_["prev_xh_store"] = tuple(h_stores)
        st_["built"] = True

    def attn_phase(astack):
        qfm = b.sb(None, [128, 8, 2048], BF16, astack)
        kk = b.sb(None, [128, 16, 16, 64], BF16, astack)
        vv = b.sb(None, [128, 16, 8, 130], BF16, astack)
        at = b.sb(None, [128, 8, 2048], BF16, astack)
        xn2fm = at

        stackP = ExitStack()
        wTsA = b.sb(None, [128, 8, 1024], BF16, stackP)
        stackP2 = ExitStack()
        wTsB = b.sb(None, [128, 8, 1024], BF16, stackP2)
        scr1 = b.sb("scr1", [128, D], F32, stackP2)

        def load_wT(buf, wi, reads):
            return tuple(
                b.dma("sync", buf[:, cc, :],
                      wbf_dram[wi * D:(wi + 1) * D, cc * 128:(cc + 1) * 128],
                      reads=reads, transpose=True)
                for cc in range(8))

        b.phase('attn-ln2')
        qfmx = qfm.bitcast(F32)
        kkxs = kk.bitcast(F32)

        def ln2_page(gc):
            if gc < 8:
                return qfmx[:, gc, :]
            j = gc - 8
            return kkxs[:, 2 * j:2 * j + 2, :, :].rearrange(
                "p a b c -> p (a b c)")

        ln2_tx = [ln_load(x_dram, gc, ln2_page(gc), st_["ln2_stage_gate"],
                          "scalar" if gc < 8 else "gpsimd")
                  for gc in range(16)]
        stage_tok = [None] * 16
        xn2_tok = [None] * 16
        twt_cur = None
        soft_pe = ("pe", b.cnt["pe"], "_")
        for gc in range(16):
            if gc == 8:
                twt_cur = load_wT(wTsA, 0, (t_wstash["q"],))
            mv, trs = st_["ln2_pre"][gc]
            sl = gc % 2
            txn, _ = ln_apply(sl, gc, mv, trs, ln2_tx[gc], ln2_page(gc),
                              False, lambda s: qb[:, s, :], (st_["qb"][sl],))
            stage_tok[gc] = txn
            st_["qb"][sl] = txn
            t_xn_last = transpose8(
                lambda c3, s=sl: qb[:, s, c3 * 128:(c3 + 1) * 128],
                xn2fm[:, :, gc * 128:(gc + 1) * 128], (txn, soft_pe))
            st_["qb"][sl] = t_xn_last
            xn2_tok[gc] = t_xn_last
        st_["qfm_gate"] = ("dv", b.cnt["dv"], "_")
        st_["qb"][2] = stage_tok[15]

        kk_tok = [None] * 16
        vv_tok = [None] * 16
        t_vones = b.op("vector", lambda E: E.memset(vv[:, :, :, 128:130], 1.0),
                       (("pe", b.cnt["pe"], "_"),), "dv")
        b.phase('attn-proj')
        def qpage(sl):
            if sl < 2:
                return qb[:, sl, :]
            return kk[:, 15, :, :].rearrange("p h e -> p (h e)")

        def q_flush():
            psl_, ptc_, pt7_ = st_["q_pend"].pop(0)
            extra = ()
            if st_["qfm_gate"] is not None:
                extra = (st_["qfm_gate"],)
                st_["qfm_gate"] = None
            tev = transpose8(
                lambda c3, s=psl_: qpage(s)[:, c3 * 128:(c3 + 1) * 128],
                qfm[:, :, ptc_ * 128:(ptc_ + 1) * 128], (pt7_,) + extra)
            st_["qb"][psl_] = tev

        qps = [ps_b[:, 0, :], ps_b[:, 1, :], ps_a[:, 0:2, :].rearrange("p a b -> p (a b)")]
        qps_tok = [(st_["pb"][0], st_["pb"][1]),
                   (st_["pb"][2], st_["pb"][3]),
                   (st_["pa"][0], st_["pa"][1])]
        proj_pe_last = None
        for wname in ("q", "k", "v"):
            if wname == "q":
                wTs, twt = wTsA, twt_cur
                twt_next = load_wT(wTsB, 1, (t_wstash["k"],))
            elif wname == "k":
                wTs, twt = wTsB, twt_next
                twt_next = load_wT(wTsA, 2,
                                   (t_wstash["v"],
                                    ("pe", b.cnt["pe"], "_")))
            else:
                wTs, twt = wTsA, twt_next
            for tc in range(16):
                rb = tc % 3
                mm = None
                for jb in range(2):
                    for c in range(8):
                        reads = ()
                        if jb == 0 and c == 0:
                            reads = twt + (xn2_tok[t2 if False else tc], qps_tok[rb])
                        mm = b.op("tensor", lambda E, c3=c, j2=jb, t2=tc, r=rb, w=wTs:
                                  E.matmul(qps[r][:, j2 * 512:(j2 + 1) * 512],
                                           xn2fm[:, c3, t2 * 128:(t2 + 1) * 128],
                                           w[:, c3, j2 * 512:(j2 + 1) * 512],
                                           start=(c3 == 0), stop=(c3 == 7)),
                                  reads, "pe" if (jb == 1 and c == 7) else None)
                proj_pe_last = mm
                if wname == "q":
                    t4 = b.op("scalar", lambda E, r=rb: E.activation(
                        out=scr1[:], in_=qps[r][:, :], func=AF.Exp,
                        scale=SCALE_Q), (mm, st_["scr1"], t_n2), "ac")
                    qps_tok[rb] = t4
                    t5 = b.op("vector", lambda E: E.reduce_sum(
                        out=stat[:, 5, :].unsqueeze(2),
                        in_=scr1[:, :].rearrange("p (h e) -> p h e", e=64),
                        axis=AX.X), (t4,), "dv")
                    b.op("vector", lambda E: E.reciprocal(
                        out=stat[:, 5, :], in_=stat[:, 5, :]), (), "dv")
                    sl = tc % 3
                    t7 = b.op("vector", lambda E, s=sl: E.tensor_tensor(
                        out=qpage(s).rearrange("p (h e) -> p h e", e=64),
                        in0=scr1[:, :].rearrange("p (h e) -> p h e", e=64),
                        in1=stat[:, 5, :].unsqueeze(2).broadcast_to([128, 16, 64]),
                        op=ALU.mult), (st_["qb"][sl],), "dv")
                    st_["scr1"] = t7
                    st_["q_pend"].append((sl, tc, t7))
                    if len(st_["q_pend"]) == 3:
                        q_flush()
                elif wname == "k":
                    while st_["q_pend"]:
                        q_flush()
                    sl = tc % 2
                    tmx = b.op("vector", lambda E, r=rb, s=sl: E.tensor_scalar(
                        out=qb[:, s, :], in0=qps[r][:, :], scalar1=SCALE_Q,
                        scalar2=0.0, op0=ALU.mult, op1=ALU.max),
                        (mm, st_["qb"][sl]), "dv")
                    te = b.op("scalar", lambda E, r=rb: E.activation(
                        out=scr1[:], in_=qps[r][:, :], func=AF.Exp,
                        scale=SCALE_Q), (tmx, st_["scr1"]), "ac")
                    qps_tok[rb] = te
                    t4_reads = (te, tmx, stage_tok[8 + tc // 2])
                    if tc == 15:
                        t4_reads = t4_reads + (st_["qb"][2],)
                    t4 = b.op("vector", lambda E, t2=tc, s=sl: E.scalar_tensor_tensor(
                        out=kk[:, t2, :, :].rearrange("p h e -> p (h e)"),
                        in0=scr1[:], scalar=1.0, in1=qb[:, s, :],
                        op0=ALU.min, op1=ALU.add), t4_reads, "dv")
                    st_["scr1"] = t4
                    st_["qb"][sl] = t4
                    kk_tok[tc] = t4
                else:
                    tv = b.op("scalar", lambda E, t2=tc, r=rb: E.activation(
                        out=vv[:, t2, :, 0:128], in_=qps[r][:, :]
                        .rearrange("p (h e) -> p h e", e=128),
                        func=AF.Copy), (mm,), "ac")
                    qps_tok[rb] = tv
                    vv_tok[tc] = tv
        st_["pb"][0] = qps_tok[0]
        st_["pb"][1] = qps_tok[0]
        st_["pb"][2] = qps_tok[1]
        st_["pb"][3] = qps_tok[1]
        st_["pa"][0] = qps_tok[2]
        st_["pa"][1] = qps_tok[2]
        tok_vdone = ("pe", b.cnt["pe"], "_")
        stackP2.close()

        tok_proj = ("dv", b.cnt["dv"], "_")
        tok_at0 = b.op("vector", lambda E: E.memset(at[:, :, 0:64], 0.0),
                       (("pe", b.cnt["pe"], "_"),), "dv")

        stackE = ExitStack()
        cxa2 = b.sb(None, [128, 2, 31, 129], F32, stackE)
        cxb = b.sb(None, [128, 2, 31, 128], BF16, stackE)
        wTso = wTsA
        twtO = load_wT(wTsA, 3, (t_wstash["o"], tok_vdone))

        def e1_pair(p):
            cxa = cxa2[:, p % 2]
            order = list(range(0, 31, 2)) + list(range(1, 31, 2))
            evs = []
            pos = 0
            while pos < 31:
                nwave = min(9, 31 - pos)
                if pos < 16 and pos + nwave > 16:
                    nwave = 16 - pos
                wave = order[pos:pos + nwave]
                mm2 = None
                tcb_max = max(wave) // 2
                hb = (wave[0] % 2) * 64
                b.pe_guard(hb, 128, 0)
                for i, uu in enumerate(wave):
                    bank = i // 3
                    slot = i % 3
                    tcb = uu // 2
                    reads = ()
                    if i == 0:
                        reads = (kk_tok[tcb_max], vv_tok[tcb_max], t_vones,
                                 st_["pa"][0], st_["pa"][1],
                                 st_["pa"][2], st_["cxa_free"][p % 2])
                    mm2 = b.op("tensor", lambda E, bk=bank, so=slot, tb=tcb, h=hb, pp=p:
                         E.matmul(ps_a[:, bk, so * 136:so * 136 + 129],
                                  kk[h:h + 64, tb, 2 * pp:2 * pp + 2, :]
                                  .rearrange("p h e -> p (h e)"),
                                  vv[h:h + 64, tb, pp, 0:129],
                                  start=True, stop=True), reads, "pe")
                nb_banks = (nwave + 2) // 3
                for bk in range(nb_banks):
                    nin = min(3, nwave - 3 * bk)
                    U0 = wave[3 * bk]
                    ev_reads = (mm2,)
                    if p == 0 and pos == 0 and bk == 0:
                        ev_reads = (mm2, tok_vdone)
                    tevw = b.op("scalar", lambda E, u0=U0, nn=nin, bkk=bk,
                                cx=cxa:
                                E.activation(
                                    out=cx[:, u0:u0 + 2 * nn - 1:2, :],
                                    in_=ps_a[:, bkk, 0:136 * nn]
                                    .rearrange("p (s e) -> p s e", e=136)[:, :, 0:129],
                                    func=AF.Copy),
                                ev_reads, "ac")
                    st_["pa"][bk] = tevw
                    evs.append(tevw)
                pos += nwave
            return evs[-1]

        def chain_pair(p, ev_tok):
            cxa = cxa2[:, p % 2]

            def sadd(dst, src, n, step, reads=()):
                return b.op("vector", lambda E, cx=cxa, a=dst, bb=src, nn=n,
                            s=step:
                            E.tensor_tensor(
                                out=cx[:, a:a + (nn - 1) * s + 1:s, :],
                                in0=cx[:, a:a + (nn - 1) * s + 1:s, :],
                                in1=cx[:, bb:bb + (nn - 1) * s + 1:s, :],
                                op=ALU.add), reads, "dv")
            sadd(1, 0, 15, 2, (ev_tok,))
            sadd(3, 1, 7, 4)
            sadd(7, 3, 3, 8)
            sadd(15, 7, 1, 16)
            sadd(23, 15, 1, 16)
            sadd(11, 7, 3, 8)
            sadd(5, 3, 7, 4)
            tg = sadd(2, 1, 15, 2)
            tr1 = b.op("vector", lambda E, cx=cxa: E.tensor_scalar(
                out=rcpt[:, :], in0=cx[:, :, 128:129].squeeze(2),
                scalar1=EPS_ATTN, scalar2=None, op0=ALU.add), (tg,), "dv")
            b.op("vector", lambda E: E.reciprocal(
                out=rcpt[:, :], in_=rcpt[:, :]), (), "dv")
            tdv0 = b.op("vector", lambda E, cx=cxa, pp=p % 2: E.tensor_tensor(
                out=cxb[:, pp, 0:16, :], in0=cx[:, 0:16, 0:128],
                in1=rcpt[:, 0:16].unsqueeze(2).broadcast_to([128, 16, 128]),
                op=ALU.mult), (st_["cxb_free"][p % 2],), "dv")
            tdv1 = b.op("vector", lambda E, cx=cxa, pp=p % 2: E.tensor_tensor(
                out=cxb[:, pp, 16:31, :], in0=cx[:, 16:31, 0:128],
                in1=rcpt[:, 16:31].unsqueeze(2).broadcast_to([128, 15, 128]),
                op=ALU.mult), (), "dv")
            st_["cxa_free"][p % 2] = tdv1
            return (tdv0, tdv1)

        def e2_pair(p, tdv):
            ev2 = None
            for grp in range(2):
                us = list(range(1, 17)) if grp == 0 else list(range(17, 32))
                rb = grp
                mm = None
                for h2 in (0, 1):
                    hb = h2 * 64
                    b.pe_guard(hb, 64, hb)
                    for uu in us:
                        reads = ()
                        if uu == us[0] and h2 == 0:
                            reads = (tdv[grp], st_["pb"][2 * rb],
                                     st_["pb"][2 * rb + 1])
                        mm = b.op("tensor", lambda E, U=uu, h=hb, pp=p, r=rb:
                                  E.matmul(ps_b[h:h + 64, r,
                                                ((U - 1) % 16) * 64:((U - 1) % 16) * 64 + 64],
                                           cxb[h:h + 64, pp % 2, U - 1, h:h + 64],
                                           qfm[h:h + 64, pp, U * 64:U * 64 + 64],
                                           start=True, stop=True),
                                  reads, "pe")
                st_["e2_pe"] = mm
                st_["cxb_free"][p % 2] = mm
                n_u = len(us)
                ev2 = b.op("scalar", lambda E, g=grp, pp=p, r=rb, n=n_u:
                           E.activation(
                               out=at[:, pp, (16 * g + 1) * 64:(16 * g + 1 + n) * 64],
                               in_=ps_b[:, r, 0:n * 64], func=AF.Copy),
                           (mm, tok_at0), "ac")
                if grp == 0:
                    at_g0[p] = ev2
                st_["pb"][2 * rb] = ev2
                st_["pb"][2 * rb + 1] = ev2
            return ev2
        at_g0 = [None] * 8

        st_["cxa_free"] = [None, None]
        st_["cxb_free"] = [None, None]
        st_["e2_pe"] = None
        ev = e1_pair(0)
        tdv = chain_pair(0, ev)
        at_done = None
        for p in range(1, 8):
            ev = e1_pair(p)
            tdv_next = chain_pair(p, ev)
            at_done = e2_pair(p - 1, tdv)
            tdv = tdv_next
        b.phase('attn-outp')
        wTs = wTso
        twt = twtO
        ops_ = [ps_b[:, 0, :], ps_b[:, 1, :],
                ps_a[:, 0:2, :].rearrange("p a b -> p (a b)")]
        xout = [cxa2[:, pg, :, :].rearrange("p a b -> p (a b)")[:, 0:1024]
                for pg in range(2)]
        xout_tok = [st_["cxa_free"][0], st_["cxa_free"][1]]
        q3 = ("sync", "scalar", "gpsimd")
        ch0 = chunk_ap(x_dram, 0)
        tx0 = b.dma("sync", xst[:, 0, :], ch0,
                    reads=(st_["xst"][0], st_["x_load_extra"][0]))
        b.pe_guard(0, 128, 0)
        for jb in range(2):
            b.op("tensor", lambda E, j2=jb: E.matmul(
                ops_[2][:, j2 * 512:(j2 + 1) * 512], onesrow[0:1, :],
                bo_b[0:1, j2 * 512:(j2 + 1) * 512], start=True, stop=False),
                twt + (st_["pa"][0], st_["pa"][1], st_["pa"][2],
                       t_bo, t_or, tok_at0) if jb == 0 else (),
                None)
            for c in range(7):
                b.op("tensor", lambda E, c3=c, j2=jb, w=wTs:
                     E.matmul(ops_[2][:, j2 * 512:(j2 + 1) * 512],
                              at[:, c3, 0:128],
                              w[:, c3, j2 * 512:(j2 + 1) * 512],
                              start=False, stop=False),
                     (at_g0[c],) if jb == 0 else (), None)
        at_done = e2_pair(7, tdv)
        b.pe_guard(0, 128, 0)
        mm = None
        for jb in range(2):
            mm = b.op("tensor", lambda E, j2=jb, w=wTs:
                      E.matmul(ops_[2][:, j2 * 512:(j2 + 1) * 512],
                               at[:, 7, 0:128],
                               w[:, 7, j2 * 512:(j2 + 1) * 512],
                               start=False, stop=True),
                      (at_g0[7],) if jb == 0 else (),
                      "pe" if jb == 1 else None)
        tadd = b.op("vector", lambda E, xo=xout[0]: E.tensor_tensor(
            out=xo, in0=ops_[2][:, :], in1=xst[:, 0, :],
            op=ALU.add), (mm, tx0, xout_tok[0]), "dv")
        st_["xst"][0] = tadd
        tst0 = b.dma(q3[1], ch0, xout[0], reads=(tadd,))
        xout_tok[0] = tst0
        st_["prev_store"] = tst0
        st_["x_load_extra"][0] = tst0
        ops_tok = [(st_["pb"][0], st_["pb"][1]), (st_["pb"][2], st_["pb"][3]),
                   tadd]
        for tc in range(1, 16):
            sl = tc % 2
            ch = chunk_ap(x_dram, tc)
            tx = b.dma(q3[(2 * tc) % 3], xst[:, sl, :], ch,
                       reads=(st_["xst"][sl], st_["x_load_extra"][tc]))
            rb = (tc + 2) % 3
            mm = None
            for jb in range(2):
                b.op("tensor", lambda E, j2=jb, r=rb: E.matmul(
                    ops_[r][:, j2 * 512:(j2 + 1) * 512], onesrow[0:1, :],
                    bo_b[0:1, j2 * 512:(j2 + 1) * 512], start=True, stop=False),
                    twt + (at_done, ops_tok[rb], t_bo, t_or) if jb == 0 else (),
                    None)
                for c in range(8):
                    mm = b.op("tensor", lambda E, c3=c, j2=jb, t2=tc, r=rb, w=wTs:
                              E.matmul(ops_[r][:, j2 * 512:(j2 + 1) * 512],
                                       at[:, c3, t2 * 128:(t2 + 1) * 128],
                                       w[:, c3, j2 * 512:(j2 + 1) * 512],
                                       start=False, stop=(c3 == 7)),
                              (), "pe" if (jb == 1 and c == 7) else None)
            tadd = b.op("vector", lambda E, s=sl, r=rb, xo=xout[sl]:
                        E.tensor_tensor(
                            out=xo, in0=ops_[r][:, :], in1=xst[:, s, :],
                            op=ALU.add), (mm, tx, xout_tok[sl]), "dv")
            ops_tok[rb] = tadd
            st_["xst"][sl] = tadd
            tstore = b.dma(q3[(2 * tc + 1) % 3], ch, xout[sl], reads=(tadd,))
            xout_tok[sl] = tstore
            st_["prev_store"] = tstore
            st_["x_load_extra"][tc] = tstore
        st_["pb"][0] = ops_tok[0]
        st_["pb"][1] = ops_tok[0]
        st_["pb"][2] = ops_tok[1]
        st_["pb"][3] = ops_tok[1]
        st_["pa"][0] = ops_tok[2]
        st_["pa"][1] = ops_tok[2]
        stackE.close()
        stackP.close()
        st_["prev_l2_pe"] = st_["e2_pe"]

    st_["ln2_pre"] = [None] * 16

    def ln2_hook(gc, page, gate):
        if page is None:
            sl = gc % 2
            tx = b.dma("scalar", xst[:, sl, :], chunk_ap(x_dram, gc),
                       reads=(st_["xst"][sl], st_["x_load_extra"][gc]))
            pg = xst[:, sl, :]
            r = (tx,)
        else:
            pg = page
            r = gate
        b1 = b.op("vector", lambda E, p=pg: E.bn_stats(
            out=stat[:, 0, 0:6], in_=p[:, 0:512]), r)
        b2 = b.op("vector", lambda E, p=pg: E.bn_stats(
            out=stat[:, 0, 6:12], in_=p[:, 512:1024]), r)
        mrow = 1 if gc < 8 else 6
        mv = stat[:, mrow, 2 * (gc % 8):2 * (gc % 8) + 2]
        ag = b.op("vector", lambda E, m=mv: E.bn_aggr(
            out=m, in_=stat[:, 0, 0:12]), (b1, b2))
        tln = b.op("scalar", lambda E, m=mv, cc=gc: E.activation(
            out=stat[:, 7, cc:cc + 1], in_=m[:, 1:2],
            func=AF.Ln, bias=epsc[:, 0:1]), (ag, t_ep))
        trs = b.op("scalar", lambda E, cc=gc: E.activation(
            out=stat[:, 3, cc:cc + 1], in_=stat[:, 7, cc:cc + 1],
            func=AF.Exp, scale=-0.5), (tln,))
        if page is None:
            st_["xst"][gc % 2] = (b1, b2)
        st_["ln2_pre"][gc] = (mv, trs)

    n_ph = len(debug_phases)
    assert "ffn1" in debug_phases
    if n_ph > 1:
        st_["prev_xh_store"] = tuple(
            [(ds, b.cnt[ds], "_") for ds in b.dma_sems if b.cnt[ds]]
            + [("ac", b.cnt["ac"], "_")])
        st_["wprep_fn"] = wprep
    with ExitStack() as fstack:
        ffn_phase(x_in, x_dram if n_ph > 1 else out_d, fstack,
                  l2_hook=ln2_hook if "attn" in debug_phases else None)
    st_["ln2_stage_gate"] = ("dv", b.cnt["dv"], "_")
    if "attn" in debug_phases:
        b.fence()
        with ExitStack() as astack:
            attn_phase(astack)
    if "ffn2" in debug_phases:
        b.phase('ffn2')
        b.fence()
        with ExitStack() as fstack:
            ffn_phase(x_dram, out_d, fstack)
    elif n_ph > 1:
        for gc in range(16):
            sl = gc % 2
            tx = b.dma("sync", xst[:, sl, :], chunk_ap(x_dram, gc),
                       reads=(st_["xst"][sl], st_["prev_store"]))
            tst = b.dma("sync", chunk_ap(out_d, gc), xst[:, sl, :], reads=(tx,))
            st_["xst"][sl] = tst

    for ds in b.dma_sems:
        if b.cnt[ds]:
            b.streams["sync"].append(("w", ds, b.cnt[ds]))
    b.replay()
    return b


_PHASES = ("ffn1", "attn", "ffn2")


def kernel(L0_attn_x, L0_attn_Wq, L0_attn_Wk, L0_attn_Wv, L0_attn_Wo,
           L0_attn_bo, L0_ffn_k0, L0_ffn_k1, L0_ffn_k2,
           L0_ffn_s0, L0_ffn_s1, L0_ffn_s2):
    x = np.ascontiguousarray(np.asarray(L0_attn_x, dtype=np.float32))
    k0 = np.ascontiguousarray(np.asarray(L0_ffn_k0)).view(np.uint8)
    k1 = np.ascontiguousarray(np.asarray(L0_ffn_k1)).view(np.uint8)
    k2 = np.ascontiguousarray(np.asarray(L0_ffn_k2)).view(np.uint8)
    ws = {w: np.ascontiguousarray(np.asarray(a, dtype=np.float32))
          for w, a in zip("qkvo", (L0_attn_Wq, L0_attn_Wk, L0_attn_Wv, L0_attn_Wo))}
    bo = np.asarray(L0_attn_bo, dtype=np.float32).reshape(1, D)
    sv = np.ascontiguousarray(np.tile(
        np.stack([np.asarray(L0_ffn_s0, dtype=np.float32),
                  np.asarray(L0_ffn_s1, dtype=np.float32),
                  np.asarray(L0_ffn_s2, dtype=np.float32)]).reshape(1, 3),
        (128, 1)))
    ident = np.eye(128, dtype=np.float32).astype(ml_dtypes.bfloat16)

    b = build_program(_PHASES)
    in_maps = []
    for i in range(NCORES):
        in_maps.append(dict(
            x_in=x[i], k0=k0, k1=k1, k2=k2,
            Wq=ws["q"], Wk=ws["k"], Wv=ws["v"], Wo=ws["o"],
            bo=bo, sv=sv, ident=ident))
    res = run_bass_kernel_spmd(b.nc, in_maps, core_ids=list(range(NCORES)))
    return np.stack([r["out"] for r in res.results], axis=0)


# version 46
# speedup vs baseline: 1.0343x; 1.0343x over previous
import numpy as np
import ml_dtypes
from contextlib import ExitStack

import concourse.bass as bass
import concourse.mybir as mybir
from concourse.bass_utils import run_bass_kernel_spmd

F32 = mybir.dt.float32
BF16 = mybir.dt.bfloat16
U8 = mybir.dt.uint8
AF = mybir.ActivationFunctionType
ALU = mybir.AluOpType
AX = mybir.AxisListType

ENGINES = ("tensor", "vector", "scalar", "gpsimd", "sync")
SCALE_Q = float(64.0 ** (-0.25))
EPS_LN = 1e-5
EPS_ATTN = 1e-6
NCORES = 8
T = 2048
D = 1024
HID = 4096


class B:

    def __init__(self):
        import os
        self.nc = bass.Bass("TRN2", target_bir_lowering=False,
                            detect_race_conditions=bool(os.environ.get("RACEDET")))
        self.ctx = ExitStack()
        self.streams = {e: [] for e in ENGINES}
        self.cnt = {}
        self.semh = {}
        self.dma_sems = [f"q{qn}{i}" for qn in "sgav" for i in range(8)]
        for s in ("pe", "dv", "ac", "gp", "sy", *self.dma_sems):
            self.semh[s] = self.ctx.enter_context(self.nc.semaphore(s))
            self.cnt[s] = 0
        self.dmacnt = {"qs": 0, "qg": 0, "qa": 0, "qv": 0}
        self.tcount = 0
        self.phase_marks = []
        self.eng_sem = dict(tensor="pe", vector="dv", scalar="ac",
                            gpsimd="gp", sync="sy")

    _sbn = 0

    def sb(self, name, shape, dtype, stack=None):
        if name is None:
            B._sbn += 1
            name = f"sb{B._sbn}"
        return (stack or self.ctx).enter_context(
            self.nc.sbuf_tensor(name, shape, dtype))

    def ps(self, name, shape, dtype=F32):
        return self.ctx.enter_context(self.nc.psum_tensor(name, shape, dtype))

    def op(self, eng, fn, reads=(), inc=None, amt=1):
        if inc is None and eng in ("vector", "scalar", "gpsimd"):
            inc = self.eng_sem[eng]
        st = self.streams[eng]
        flat = []
        for tk in reads:
            if tk is None:
                continue
            if isinstance(tk, tuple) and len(tk) == 3 and isinstance(tk[0], str):
                flat.append(tk)
            else:
                flat.extend(t for t in tk if t is not None)
        for tk in flat:
            s, v, peng = tk
            if peng == eng == "tensor":
                continue
            assert v <= self.cnt[s], f"future token {s}={v} > {self.cnt[s]}"
            st.append(("w", s, v))
        if eng in ("vector", "scalar", "gpsimd"):
            own = self.eng_sem[eng]
            if self.cnt[own]:
                st.append(("w", own, self.cnt[own]))
        if eng == "tensor":
            self.tcount += 1
        tok = None
        if inc is not None:
            self.cnt[inc] += amt
            tok = (inc, self.cnt[inc], eng)
        st.append(("i", fn, inc, amt if inc else 0))
        return tok

    def dma(self, eng, out, in_, reads=(), transpose=False):
        assert eng in ("sync", "gpsimd", "scalar", "vector")
        q = {"sync": "qs", "gpsimd": "qg", "scalar": "qa", "vector": "qv"}[eng]
        n = self.dmacnt[q]
        self.dmacnt[q] += 1
        sem = q[1] and f"q{q[1]}{n % 8}"
        rounds = n // 8
        reads = tuple(reads)
        if rounds > 0:
            reads = reads + ((sem, 16 * rounds, "_"),)
        return self.op(eng, lambda E, o=out, i=in_, tr=transpose:
                       E.dma_start(out=o, in_=i, transpose=tr),
                       reads, sem, 16)

    _pe_base = (0,)

    def phase(self, name):
        self.phase_marks.append((name, self.tcount))

    def pe_guard(self, *cfg):
        if cfg != self._pe_base:
            if self.cnt["pe"]:
                self.streams["tensor"].append(("w", "pe", self.cnt["pe"]))
            self._pe_base = cfg

    def fence(self):
        toks = [(se, self.cnt[se], "_") for se in ("pe", "dv", "ac", "gp")
                if self.cnt[se]]
        for ds in self.dma_sems:
            if self.cnt[ds]:
                toks.append((ds, self.cnt[ds], "_"))
        for e in ENGINES:
            own = self.eng_sem[e]
            for tk in toks:
                if tk[0] == own:
                    continue
                self.streams[e].append(("w", tk[0], tk[1]))

    def replay(self):
        nc = self.nc
        with nc.Block() as blk:
            def mk(eng):
                def body(E):
                    seen = {}
                    nw = 0
                    for item in self.streams[eng]:
                        if item[0] == "w":
                            _, s, v = item
                            if seen.get(s, 0) >= v:
                                continue
                            seen[s] = v
                            if nw >= 2:
                                E.nop(nofuse=True)
                                nw = 0
                            E.wait_ge(self.semh[s], v)
                            nw += 1
                        else:
                            nw = 0
                            _, fn, inc, amt = item
                            inst = fn(E)
                            if inc is not None and amt:
                                inst.then_inc(self.semh[inc], amt)
                return body
            blk.tensor(mk("tensor"))
            blk.vector(mk("vector"))
            blk.scalar(mk("scalar"))
            blk.gpsimd(mk("gpsimd"))
            blk.sync(mk("sync"))


def chunk_ap(dram_ap, gc):
    return dram_ap.rearrange("(n p) d -> n p d", p=128)[gc]


def build_program(debug_phases=("ffn1", "attn", "ffn2")):
    b = B()
    nc = b.nc

    x_in = nc.dram_tensor("x_in", [T, D], F32, kind="ExternalInput").ap()
    k0_d = nc.dram_tensor("k0", [2 * D, HID], U8, kind="ExternalInput").ap()
    k1_d = nc.dram_tensor("k1", [HID, HID], U8, kind="ExternalInput").ap()
    k2_d = nc.dram_tensor("k2", [HID, D], U8, kind="ExternalInput").ap()
    w_d = {w: nc.dram_tensor("W" + w, [D, D], F32, kind="ExternalInput").ap()
           for w in ("q", "k", "v", "o")}
    bo_d = nc.dram_tensor("bo", [1, D], F32, kind="ExternalInput").ap()
    sv_d = nc.dram_tensor("sv", [128, 3], F32, kind="ExternalInput").ap()
    id_d = nc.dram_tensor("ident", [128, 128], BF16, kind="ExternalInput").ap()
    out_d = nc.dram_tensor("out", [T, D], F32, kind="ExternalOutput").ap()

    x_dram = nc.dram_tensor("x_scr", [T, D], F32, kind="Internal").ap()
    d0_dram = nc.dram_tensor("d0_scr", [D, HID], BF16, kind="Internal").ap()
    k1_pm = nc.dram_tensor("k1pm_scr", [HID, HID], BF16, kind="Internal").ap()
    k2_pm = nc.dram_tensor("k2pm_scr", [HID, D], BF16, kind="Internal").ap()
    wbf_dram = nc.dram_tensor("wbf_scr", [4 * D, D], BF16, kind="Internal").ap()

    idn = b.sb("idn", [128, 128], BF16)
    scol = b.sb("scol", [128, 3], F32)
    deriv = b.sb("deriv", [128, 3], F32)
    onescol = b.sb("onescol", [128, 1], BF16)
    onesrow = b.sb("onesrow", [1, 128], BF16)
    bo_b = b.sb("bo_b", [1, D], BF16)
    xst = b.sb("xst", [128, 2, D], F32)
    qb = b.sb("qb", [128, 2, D], BF16)
    stat = b.sb("stat", [128, 8, 16], F32)
    rcpt = b.sb("rcpt", [128, 31], F32)
    epsc = b.sb("epsc", [128, 1], F32)
    neg2 = b.sb("neg2", [128, 1], F32)

    ps_a = b.ps("ps_a", [128, 3, 512])
    ps_b = b.ps("ps_b", [128, 2, 1024])
    ps_t = b.ps("ps_t", [128, 8, 128], BF16)
    pbt = [ps_b[:, n // 2, (n % 2) * 512:(n % 2 + 1) * 512] for n in range(4)]
    tile7 = pbt + [ps_a[:, i, :] for i in range(3)]

    def t7_tok(i):
        return st_["pb"][i] if i < 4 else st_["pa"][i - 4]

    def t7_set(i, tok):
        if i < 4:
            st_["pb"][i] = tok
        else:
            st_["pa"][i - 4] = tok

    st_ = {
        "pa": [None, None, None],
        "pb": [None, None, None, None],
        "pt": None,
        "xst": [None, None],
        "qb": [None, None, None],
        "qfm_gate": None,
        "scr1": None, "scr2": None,
        "mv": [None, None],
        "q_pend": [],
        "wstq": [None, None, None, None],
        "rot7": 0,
        "x_load_extra": [None] * 16,
        "slot_stash": [None, None, None, None],
        "built": False,
        "t_w_stash": None,
        "prev_l2_pe": None,
        "prev_store": None,
        "prev_xh_store": None,
    }

    t_id = b.dma("sync", idn[:], id_d)
    t_bo = b.dma("gpsimd", bo_b[:], bo_d)
    t_oc = b.op("vector", lambda E: E.memset(onescol[:], 1.0), (), "dv")
    t_or = b.op("vector", lambda E: E.memset(onesrow[:], 1.0), (), "dv")
    t_ep = b.op("vector", lambda E: E.memset(epsc[:], EPS_LN), (), "dv")
    t_n2 = b.op("vector", lambda E: E.memset(neg2[:], -2.0), (), "dv")
    t_scol = b.dma("gpsimd", scol[:, :], sv_d)
    t_d0c = b.op("vector", lambda E: E.tensor_scalar(
        out=deriv[:, 0:1], in0=scol[:, 0:1], scalar1=4.0, scalar2=None,
        op0=ALU.mult), (t_scol,), "dv")
    t_d1c = b.op("vector", lambda E: E.tensor_scalar(
        out=deriv[:, 1:2], in0=scol[:, 1:2], scalar1=2.0, scalar2=None,
        op0=ALU.mult), (t_scol,), "dv")
    t_d2c = b.op("vector", lambda E: E.tensor_scalar(
        out=deriv[:, 2:3], in0=scol[:, 2:3], scalar1=0.5, scalar2=None,
        op0=ALU.mult), (t_scol,), "dv")

    t_wstash = {}

    def wprep():
        for wi, wname in enumerate(("q", "k", "v", "o")):
            t_wstash[wname] = b.dma(
                "gpsimd", wbf_dram[wi * D:(wi + 1) * D, :], w_d[wname],
                reads=())


    def ln_load(src_ap, gc, page, pg_tok, q):
        return b.dma(q, page, chunk_ap(src_ap, gc),
                     reads=(pg_tok, st_["x_load_extra"][gc]))

    def ln_stats(src_ap, gc, c, page=None, tx_pre=None):
        sl = gc % 2
        ch = chunk_ap(src_ap, gc)
        if page is None:
            pg, is_xst = xst[:, sl, :], True
            gates = (st_["xst"][sl], st_["x_load_extra"][gc])
            tx_a = b.dma("sync", pg[:, 0:512], ch[:, 0:512], reads=gates)
            tx_b = b.dma("gpsimd", pg[:, 512:1024], ch[:, 512:1024],
                         reads=gates)
            tx = (tx_a, tx_b)
            r1, r2 = (tx_a,), (tx_b,)
        else:
            pg, is_xst = page, False
            tx = tx_pre
            r1 = r2 = (tx,)
        b1 = b.op("vector", lambda E, p=pg: E.bn_stats(
            out=stat[:, 0, 0:6], in_=p[:, 0:512]), r1)
        b2 = b.op("vector", lambda E, p=pg: E.bn_stats(
            out=stat[:, 0, 6:12], in_=p[:, 512:1024]), r2)
        mvsl = c % 2
        mv = stat[:, 1, 2 * mvsl:2 * mvsl + 2]
        ag = b.op("vector", lambda E, m=mv: E.bn_aggr(
            out=m, in_=stat[:, 0, 0:12]), (b1, b2, st_["mv"][mvsl]))
        tln = b.op("scalar", lambda E, m=mv, cc=c: E.activation(
            out=stat[:, 7, cc:cc + 1], in_=m[:, 1:2],
            func=AF.Ln, bias=epsc[:, 0:1]), (ag, t_ep))
        trs = b.op("scalar", lambda E, cc=c: E.activation(
            out=stat[:, 3, cc:cc + 1], in_=stat[:, 7, cc:cc + 1],
            func=AF.Exp, scale=-0.5), (tln,))
        return (sl, c, mv, trs, tx, pg, is_xst)

    def ln_apply(sl, c, mv, trs, tx, pg, is_xst, dst_fn, extra_xn_reads,
                 eng="vector"):
        dst = dst_fn(sl)
        txn = b.op(eng, lambda E, p=pg, cc=c, d=dst, m=mv: E.tensor_scalar(
            out=d, in0=p,
            scalar1=m[:, 0:1], scalar2=stat[:, 3, cc:cc + 1],
            op0=ALU.subtract, op1=ALU.mult),
            (trs, tx) + tuple(extra_xn_reads))
        st_["mv"][c % 2] = (txn,)
        if is_xst:
            st_["xst"][sl] = (txn,)
        return txn, sl

    def transpose8(src_getter, dst_ap, reads, eng="vector"):
        ttr = None
        for c2 in range(8):
            ttr = b.op("tensor", lambda E, c3=c2: E.transpose(
                ps_t[:, c3, :], src_getter(c3), idn[:]),
                tuple(reads) + (st_["pt"], t_id), "pe")
        if eng == "scalar":
            tev = b.op("scalar", lambda E, d=dst_ap: E.activation(
                out=d, in_=ps_t[:, :, :], func=AF.Copy), (ttr,), "ac")
        else:
            tev = b.op("vector", lambda E, d=dst_ap: E.tensor_copy(
                out=d, in_=ps_t[:, :, :]), (ttr,), "dv")
        st_["pt"] = tev
        return tev

    def ffn_phase(src_ap, dst_ap, fstack, l2_hook=None):
        b.pe_guard(0, 128, 0)
        a1 = b.sb(None, [128, 32, 1024], BF16, fstack)
        a2 = b.sb(None, [128, 32, 1024], BF16, fstack)
        wst = b.sb(None, [128, 4, 8, 512], BF16, fstack)
        u8a = b.sb(None, [128, 8, 512], U8, fstack)
        u8b = b.sb(None, [128, 8, 512], U8, fstack)
        xnf = b.sb(None, [128, 8, 1024], BF16, fstack)
        xh = a1[:, 0:16, :].bitcast(F32).rearrange("p (a c) n -> p a (c n)", c=2)
        a2x = a2[:, 0:16, :].bitcast(F32).rearrange("p (a c) n -> p a (c n)", c=2)

        u8_tok = [None, None]
        a1_evac = None
        a2_evac = None
        l1_done = None
        build = not st_["built"]
        l0_pe = [None, None]

        def wslot_load(qslot, view, u8_src, pm_blk, pm_src, first, sc_col):
            if first:
                tq = b.dma("gpsimd", view, u8_src,
                           reads=(st_["wstq"][qslot], st_["slot_stash"][qslot]))
                tcv = b.op("vector", lambda E, v=view: E.tensor_scalar(
                    out=v, in0=v, scalar1=2.0, scalar2=-1.0,
                    op0=ALU.mult, op1=ALU.add), (tq,), "dv")
                tst = b.dma("sync", pm_blk, view, reads=(tcv,))
                st_["t_w_stash"] = tst
                st_["slot_stash"][qslot] = tst
                return (tcv,), tst
            else:
                tq = b.dma("gpsimd", view, pm_src,
                           reads=(st_["wstq"][qslot], st_["slot_stash"][qslot],
                                  st_["t_w_stash"]))
                return (tq,), None

        def ln_half_steps(half):
            toks = []
            state = {"pend": None, "c": 0}

            def flush():
                sl, c, mv, trs, tx, pg, isx = state["pend"]
                txn, _ = ln_apply(sl, c, mv, trs, tx, pg, isx,
                                  lambda s: qb[:, s, :], (st_["qb"][sl],))
                st_["qb"][sl] = txn
                t_last = transpose8(
                    lambda c3, s=sl: qb[:, s, c3 * 128:(c3 + 1) * 128],
                    xnf[:, :, c * 128:(c + 1) * 128],
                    (txn, l0_pe[1 - half]))
                st_["qb"][sl] = t_last
                toks.append(t_last)
                state["pend"] = None

            def step():
                c = state["c"]
                if c < 8:
                    if half == 0:
                        if c == 0:
                            state["txs"] = [
                                ln_load(src_ap, cc, a2x[:, cc, :], None,
                                        ("sync", "scalar", "gpsimd")[cc % 3])
                                for cc in range(8)]
                        pend_new = ln_stats(src_ap, c, c, page=a2x[:, c, :],
                                            tx_pre=state["txs"][c])
                    else:
                        pend_new = ln_stats(src_ap, 8 + c, c)
                    if state["pend"] is not None:
                        flush()
                    state["pend"] = pend_new
                elif state["pend"] is not None:
                    flush()
                state["c"] = c + 1
            return step, toks

        def ln_half(half):
            step, toks = ln_half_steps(half)
            for _ in range(9):
                step()
            return toks

        def l0_half(half, t_xn_last):
            rp = 0
            nonlocal a1_evac
            for nb in range(8):
                sl = nb % 2
                dsl = wst[:, sl, 0:8, :]
                d0blk = d0_dram[:, nb * 512:(nb + 1) * 512].rearrange(
                    "(cc p) n -> p cc n", p=128)
                if build and half == 0:
                    ta = b.dma("sync", u8a[:],
                               k0_d[0:D, nb * 512:(nb + 1) * 512]
                               .rearrange("(cc p) n -> p cc n", p=128),
                               reads=(u8_tok[0],))
                    tb = b.dma("sync", u8b[:],
                               k0_d[D:2 * D, nb * 512:(nb + 1) * 512]
                               .rearrange("(cc p) n -> p cc n", p=128),
                               reads=(u8_tok[1],))
                    td1 = b.op("vector", lambda E, ds=dsl: E.tensor_tensor(
                        out=ds, in0=u8a[:], in1=u8b[:], op=ALU.subtract),
                        (ta, tb, st_["wstq"][sl], st_["slot_stash"][sl], t_d0c),
                        "dv")
                    u8_tok[0] = td1
                    u8_tok[1] = td1
                    tslot = b.op("vector", lambda E, ds=dsl: E.tensor_scalar(
                        out=ds, in0=ds, scalar1=deriv[:, 0:1], scalar2=None,
                        op0=ALU.mult), (), "dv")
                    st_["t_w_stash"] = b.dma("sync", d0blk, dsl, reads=(tslot,))
                    st_["slot_stash"][sl] = st_["t_w_stash"]
                    tslot = (tslot, st_["t_w_stash"])
                else:
                    tslot = b.dma("sync", dsl, d0blk,
                                  reads=(st_["wstq"][sl], st_["slot_stash"][sl],
                                         st_["t_w_stash"]))
                    tslot = (tslot,)
                grp_last = None
                for n in range(4):
                    for th in range(2):
                        rp = st_["rot7"]
                        st_["rot7"] = (rp + 1) % 7
                        reads = tslot + (t_xn_last[th * 4 + 3], t7_tok(rp))
                        mm = None
                        for cch in range(8):
                            mm = b.op("tensor",
                                      lambda E, r=rp, nn=n, t2=th, c3=cch, ds=dsl:
                                      E.matmul(tile7[r],
                                               ds[:, c3, nn * 128:(nn + 1) * 128],
                                               xnf[:, c3, t2 * 512:(t2 + 1) * 512],
                                               start=(c3 == 0), stop=(c3 == 7)),
                                      reads if cch == 0 else (),
                                      "pe" if cch == 7 else None)
                        grp_last = mm
                        ev_reads = [mm]
                        if nb == 0 and n == 0 and th == 0:
                            ev_reads.append(st_["prev_xh_store"])
                        tev = b.op("scalar", lambda E, r=rp, nn=n, t2=th, nb2=nb:
                                   E.activation(out=a1[:, nb2 * 4 + nn,
                                                       t2 * 512:(t2 + 1) * 512],
                                                in_=tile7[r], func=AF.Relu),
                                   ev_reads, "ac")
                        t7_set(rp, tev)
                        a1_evac = tev
                st_["wstq"][sl] = grp_last
            l0_pe[half] = grp_last

        b.phase(f'ffn-ln0')
        t_xn = ln_half(0)
        b.phase(f'ffn-l0a')
        l0_half(0, t_xn)
        b.phase('ffn-ln1')
        ln1_step, t_xn_next = ln_half_steps(1)
        for half in range(2):
            if half == 1:
                for _ in range(9):
                    ln1_step()
                l0_half(1, t_xn_next)
            b.phase(f'ffn-l1-{half}')
            if half == 0 and build and st_.get("wprep_fn") is not None:
                st_["wprep_fn"]()
                st_["wprep_fn"] = None
            for nb in range(8):
                if half == 0:
                    ln1_step()
                qtok = [None] * 4
                for q in range(4):
                    u8src = k1_d[q * 1024:(q + 1) * 1024,
                                 nb * 512:(nb + 1) * 512].rearrange(
                        "(cc p) n -> p cc n", p=128)
                    pmblk = k1_pm[q * 1024:(q + 1) * 1024,
                                  nb * 512:(nb + 1) * 512].rearrange(
                        "(cc p) n -> p cc n", p=128)
                    qtok[q], _ = wslot_load(q, wst[:, q, :, :], u8src,
                                            pmblk, pmblk,
                                            build and half == 0, None)
                for th in range(2):
                    tl = [0, 0, 0, 0]
                    mm_last = [None] * 4
                    for n in range(4):
                        tl[n] = st_["rot7"]
                        st_["rot7"] = (st_["rot7"] + 1) % 7
                    for c in range(32):
                        q = c // 8
                        for n in range(4):
                            qmark = (th == 1 and c % 8 == 7 and n == 3)
                            reads = []
                            if n == 0 and c % 8 == 0:
                                reads.append(qtok[q])
                            if c == 0:
                                reads += [t7_tok(tl[n]), a1_evac]
                            mm = b.op("tensor",
                                      lambda E, nn=n, c3=c, t2=th, qq=q, r=tl[n]:
                                      E.matmul(tile7[r],
                                               wst[:, qq, c3 % 8, nn * 128:(nn + 1) * 128],
                                               a1[:, c3, t2 * 512:(t2 + 1) * 512],
                                               start=(c3 == 0), stop=(c3 == 31)),
                                      reads, "pe" if (c == 31 or qmark) else None)
                            if c == 31:
                                mm_last[n] = mm
                            if qmark:
                                st_["wstq"][q] = mm
                    for n in range(4):
                        tev = b.op("scalar", lambda E, nn=n, t2=th, nb2=nb, r=tl[n]:
                                   E.activation(out=a2[:, nb2 * 4 + nn,
                                                       t2 * 512:(t2 + 1) * 512],
                                                in_=tile7[r], func=AF.Relu,
                                                scale=deriv[:, 1:2]),
                                   (mm_last[n], t_d1c), "ac")
                        t7_set(tl[n], tev)
                        a2_evac = tev
            l1_done = st_["wstq"][3]
            if half == 0:
                ln1_step()

            b.phase(f'ffn-l2-{half}')
            xh_tok = [None] * 8
            for t0 in range(0, 8, 4):
                gc = half * 8 + t0
                tld = b.dma("sync", xh[:, t0:t0 + 4, :],
                            src_ap[gc * 128:(gc + 4) * 128, :]
                            .rearrange("(n p) d -> p n d", p=128),
                            reads=(l1_done, st_["prev_xh_store"],
                                   st_["prev_store"]))
                for t128 in range(t0, t0 + 4):
                    xh_tok[t128] = tld
            if half == 1 and l2_hook is not None:
                for gc in range(8):
                    l2_hook(gc, None, None)
            evac_tok = [None] * 8
            h_stores = []
            for nb in range(2):
                qtok2 = [None] * 4
                for q in range(4):
                    u8src = k2_d[q * 1024:(q + 1) * 1024,
                                 nb * 512:(nb + 1) * 512].rearrange(
                        "(cc p) n -> p cc n", p=128)
                    pmblk = k2_pm[q * 1024:(q + 1) * 1024,
                                  nb * 512:(nb + 1) * 512].rearrange(
                        "(cc p) n -> p cc n", p=128)
                    qtok2[q], _ = wslot_load(q, wst[:, q, :, :], u8src,
                                             pmblk, pmblk,
                                             build and half == 0, None)
                for t128 in range(8):
                    r3 = st_["rot7"]
                    st_["rot7"] = (r3 + 1) % 7
                    mm = None
                    for c in range(32):
                        q = c // 8
                        reads = []
                        if c % 8 == 0:
                            reads.append(qtok2[q])
                        if c == 0:
                            reads += [a2_evac, t7_tok(r3)]
                        qmark2 = (c % 8 == 7 and t128 == 7)
                        mm = b.op("tensor",
                                  lambda E, c3=c, tt=t128, qq=q, r=r3:
                                  E.matmul(tile7[r],
                                           a2[:, c3, tt * 128:(tt + 1) * 128],
                                           wst[:, qq, c3 % 8, :],
                                           start=(c3 == 0), stop=(c3 == 31)),
                                  reads, "pe" if (c == 31 or qmark2) else None)
                        if qmark2:
                            st_["wstq"][q] = mm
                    tst = b.op("vector", lambda E, tt=t128, r=r3, nb2=nb:
                               E.scalar_tensor_tensor(
                                   out=xh[:, tt, nb2 * 512:(nb2 + 1) * 512],
                                   in0=tile7[r], scalar=deriv[:, 2:3],
                                   in1=xh[:, tt, nb2 * 512:(nb2 + 1) * 512],
                                   op0=ALU.mult, op1=ALU.add),
                               (mm, t_d2c, xh_tok[t128]), "dv")
                    t7_set(r3, tst)
                    if nb == 1:
                        tstore = b.dma("sync", chunk_ap(dst_ap, half * 8 + t128),
                                       xh[:, t128, :], reads=(tst,))
                        h_stores.append(tstore)
                        st_["prev_store"] = tstore
                        st_["x_load_extra"][half * 8 + t128] = tstore
                        if half == 1 and l2_hook is not None:
                            l2_hook(8 + t128, xh[:, t128, :], (tst,))
            st_["prev_l2_pe"] = mm
            st_["prev_xh_store"] = tuple(h_stores)
        st_["built"] = True

    def attn_phase(astack):
        qfm = b.sb(None, [128, 8, 2048], BF16, astack)
        kk = b.sb(None, [128, 16, 16, 64], BF16, astack)
        vv = b.sb(None, [128, 16, 8, 130], BF16, astack)
        at = b.sb(None, [128, 8, 2048], BF16, astack)
        xn2fm = at

        stackP = ExitStack()
        wTsA = b.sb(None, [128, 8, 1024], BF16, stackP)
        stackP2 = ExitStack()
        wTsB = b.sb(None, [128, 8, 1024], BF16, stackP2)
        scr1 = b.sb("scr1", [128, D], F32, stackP2)

        def load_wT(buf, wi, reads):
            return tuple(
                b.dma("sync", buf[:, cc, :],
                      wbf_dram[wi * D:(wi + 1) * D, cc * 128:(cc + 1) * 128],
                      reads=reads, transpose=True)
                for cc in range(8))

        b.phase('attn-ln2')
        qfmx = qfm.bitcast(F32)
        kkxs = kk.bitcast(F32)

        def ln2_page(gc):
            if gc < 8:
                return qfmx[:, gc, :]
            j = gc - 8
            return kkxs[:, 2 * j:2 * j + 2, :, :].rearrange(
                "p a b c -> p (a b c)")

        ln2_tx = [ln_load(x_dram, gc, ln2_page(gc), st_["ln2_stage_gate"],
                          "scalar" if gc < 8 else "gpsimd")
                  for gc in range(16)]
        stage_tok = [None] * 16
        xn2_tok = [None] * 16
        twt_cur = None
        soft_pe = ("pe", b.cnt["pe"], "_")
        for gc in range(16):
            if gc == 8:
                twt_cur = load_wT(wTsA, 0, (t_wstash["q"],))
            mv, trs = st_["ln2_pre"][gc]
            sl = gc % 2
            txn, _ = ln_apply(sl, gc, mv, trs, ln2_tx[gc], ln2_page(gc),
                              False, lambda s: qb[:, s, :], (st_["qb"][sl],))
            stage_tok[gc] = txn
            st_["qb"][sl] = txn
            t_xn_last = transpose8(
                lambda c3, s=sl: qb[:, s, c3 * 128:(c3 + 1) * 128],
                xn2fm[:, :, gc * 128:(gc + 1) * 128], (txn, soft_pe))
            st_["qb"][sl] = t_xn_last
            xn2_tok[gc] = t_xn_last
        st_["qfm_gate"] = ("dv", b.cnt["dv"], "_")
        st_["qb"][2] = stage_tok[15]

        kk_tok = [None] * 16
        vv_tok = [None] * 16
        t_vones = b.op("vector", lambda E: E.memset(vv[:, :, :, 128:130], 1.0),
                       (("pe", b.cnt["pe"], "_"),), "dv")
        b.phase('attn-proj')
        def qpage(sl):
            if sl < 2:
                return qb[:, sl, :]
            return kk[:, 15, :, :].rearrange("p h e -> p (h e)")

        def q_flush():
            psl_, ptc_, pt7_ = st_["q_pend"].pop(0)
            extra = ()
            if st_["qfm_gate"] is not None:
                extra = (st_["qfm_gate"],)
                st_["qfm_gate"] = None
            tev = transpose8(
                lambda c3, s=psl_: qpage(s)[:, c3 * 128:(c3 + 1) * 128],
                qfm[:, :, ptc_ * 128:(ptc_ + 1) * 128], (pt7_,) + extra)
            st_["qb"][psl_] = tev

        qps = [ps_b[:, 0, :], ps_b[:, 1, :], ps_a[:, 0:2, :].rearrange("p a b -> p (a b)")]
        qps_tok = [(st_["pb"][0], st_["pb"][1]),
                   (st_["pb"][2], st_["pb"][3]),
                   (st_["pa"][0], st_["pa"][1])]
        proj_pe_last = None
        for wname in ("q", "k", "v"):
            if wname == "q":
                wTs, twt = wTsA, twt_cur
                twt_next = load_wT(wTsB, 1, (t_wstash["k"],))
            elif wname == "k":
                wTs, twt = wTsB, twt_next
                twt_next = load_wT(wTsA, 2,
                                   (t_wstash["v"],
                                    ("pe", b.cnt["pe"], "_")))
            else:
                wTs, twt = wTsA, twt_next
            for tc in range(16):
                rb = tc % 3
                mm = None
                for jb in range(2):
                    for c in range(8):
                        reads = ()
                        if jb == 0 and c == 0:
                            reads = twt + (xn2_tok[t2 if False else tc], qps_tok[rb])
                        mm = b.op("tensor", lambda E, c3=c, j2=jb, t2=tc, r=rb, w=wTs:
                                  E.matmul(qps[r][:, j2 * 512:(j2 + 1) * 512],
                                           xn2fm[:, c3, t2 * 128:(t2 + 1) * 128],
                                           w[:, c3, j2 * 512:(j2 + 1) * 512],
                                           start=(c3 == 0), stop=(c3 == 7)),
                                  reads, "pe" if (jb == 1 and c == 7) else None)
                proj_pe_last = mm
                if wname == "q":
                    t4 = b.op("scalar", lambda E, r=rb: E.activation(
                        out=scr1[:], in_=qps[r][:, :], func=AF.Exp,
                        scale=SCALE_Q), (mm, st_["scr1"], t_n2), "ac")
                    qps_tok[rb] = t4
                    t5 = b.op("vector", lambda E: E.reduce_sum(
                        out=stat[:, 5, :].unsqueeze(2),
                        in_=scr1[:, :].rearrange("p (h e) -> p h e", e=64),
                        axis=AX.X), (t4,), "dv")
                    b.op("vector", lambda E: E.reciprocal(
                        out=stat[:, 5, :], in_=stat[:, 5, :]), (), "dv")
                    sl = tc % 3
                    t7 = b.op("vector", lambda E, s=sl: E.tensor_tensor(
                        out=qpage(s).rearrange("p (h e) -> p h e", e=64),
                        in0=scr1[:, :].rearrange("p (h e) -> p h e", e=64),
                        in1=stat[:, 5, :].unsqueeze(2).broadcast_to([128, 16, 64]),
                        op=ALU.mult), (st_["qb"][sl],), "dv")
                    st_["scr1"] = t7
                    st_["q_pend"].append((sl, tc, t7))
                    if len(st_["q_pend"]) == 3:
                        q_flush()
                elif wname == "k":
                    while st_["q_pend"]:
                        q_flush()
                    sl = tc % 2
                    tmx = b.op("vector", lambda E, r=rb, s=sl: E.tensor_scalar(
                        out=qb[:, s, :], in0=qps[r][:, :], scalar1=SCALE_Q,
                        scalar2=0.0, op0=ALU.mult, op1=ALU.max),
                        (mm, st_["qb"][sl]), "dv")
                    te = b.op("scalar", lambda E, r=rb: E.activation(
                        out=scr1[:], in_=qps[r][:, :], func=AF.Exp,
                        scale=SCALE_Q), (tmx, st_["scr1"]), "ac")
                    qps_tok[rb] = te
                    t4_reads = (te, tmx, stage_tok[8 + tc // 2])
                    if tc == 15:
                        t4_reads = t4_reads + (st_["qb"][2],)
                    t4 = b.op("vector", lambda E, t2=tc, s=sl: E.scalar_tensor_tensor(
                        out=kk[:, t2, :, :].rearrange("p h e -> p (h e)"),
                        in0=scr1[:], scalar=1.0, in1=qb[:, s, :],
                        op0=ALU.min, op1=ALU.add), t4_reads, "dv")
                    st_["scr1"] = t4
                    st_["qb"][sl] = t4
                    kk_tok[tc] = t4
                else:
                    tv = b.op("scalar", lambda E, t2=tc, r=rb: E.activation(
                        out=vv[:, t2, :, 0:128], in_=qps[r][:, :]
                        .rearrange("p (h e) -> p h e", e=128),
                        func=AF.Copy), (mm,), "ac")
                    qps_tok[rb] = tv
                    vv_tok[tc] = tv
        st_["pb"][0] = qps_tok[0]
        st_["pb"][1] = qps_tok[0]
        st_["pb"][2] = qps_tok[1]
        st_["pb"][3] = qps_tok[1]
        st_["pa"][0] = qps_tok[2]
        st_["pa"][1] = qps_tok[2]
        tok_vdone = ("pe", b.cnt["pe"], "_")
        stackP2.close()

        tok_proj = ("dv", b.cnt["dv"], "_")
        tok_at0 = b.op("vector", lambda E: E.memset(at[:, :, 0:64], 0.0),
                       (("pe", b.cnt["pe"], "_"),), "dv")

        stackE = ExitStack()
        cxa2 = b.sb(None, [128, 2, 31, 129], F32, stackE)
        cxb = b.sb(None, [128, 2, 31, 128], BF16, stackE)
        wTso = wTsA
        twtO = load_wT(wTsA, 3, (t_wstash["o"], tok_vdone))

        def e1_pair(p):
            cxa = cxa2[:, p % 2]
            order = list(range(0, 31, 2)) + list(range(1, 31, 2))
            evs = []
            pos = 0
            while pos < 31:
                nwave = min(9, 31 - pos)
                if pos < 16 and pos + nwave > 16:
                    nwave = 16 - pos
                wave = order[pos:pos + nwave]
                mm2 = None
                tcb_max = max(wave) // 2
                hb = (wave[0] % 2) * 64
                b.pe_guard(hb, 128, 0)
                for i, uu in enumerate(wave):
                    bank = i // 3
                    slot = i % 3
                    tcb = uu // 2
                    reads = ()
                    if i == 0:
                        reads = (kk_tok[tcb_max], vv_tok[tcb_max], t_vones,
                                 st_["pa"][0], st_["pa"][1],
                                 st_["pa"][2], st_["cxa_free"][p % 2])
                    mm2 = b.op("tensor", lambda E, bk=bank, so=slot, tb=tcb, h=hb, pp=p:
                         E.matmul(ps_a[:, bk, so * 136:so * 136 + 129],
                                  kk[h:h + 64, tb, 2 * pp:2 * pp + 2, :]
                                  .rearrange("p h e -> p (h e)"),
                                  vv[h:h + 64, tb, pp, 0:129],
                                  start=True, stop=True), reads, "pe")
                nb_banks = (nwave + 2) // 3
                for bk in range(nb_banks):
                    nin = min(3, nwave - 3 * bk)
                    U0 = wave[3 * bk]
                    ev_reads = (mm2,)
                    if p == 0 and pos == 0 and bk == 0:
                        ev_reads = (mm2, tok_vdone)
                    tevw = b.op("scalar", lambda E, u0=U0, nn=nin, bkk=bk,
                                cx=cxa:
                                E.activation(
                                    out=cx[:, u0:u0 + 2 * nn - 1:2, :],
                                    in_=ps_a[:, bkk, 0:136 * nn]
                                    .rearrange("p (s e) -> p s e", e=136)[:, :, 0:129],
                                    func=AF.Copy),
                                ev_reads, "ac")
                    st_["pa"][bk] = tevw
                    evs.append(tevw)
                pos += nwave
            return evs[-1]

        def chain_pair(p, ev_tok):
            cxa = cxa2[:, p % 2]

            def sadd(dst, src, n, step, reads=()):
                return b.op("vector", lambda E, cx=cxa, a=dst, bb=src, nn=n,
                            s=step:
                            E.tensor_tensor(
                                out=cx[:, a:a + (nn - 1) * s + 1:s, :],
                                in0=cx[:, a:a + (nn - 1) * s + 1:s, :],
                                in1=cx[:, bb:bb + (nn - 1) * s + 1:s, :],
                                op=ALU.add), reads, "dv")
            sadd(1, 0, 15, 2, (ev_tok,))
            sadd(3, 1, 7, 4)
            sadd(7, 3, 3, 8)
            sadd(15, 7, 1, 16)
            sadd(23, 15, 1, 16)
            sadd(11, 7, 3, 8)
            sadd(5, 3, 7, 4)
            tg = sadd(2, 1, 15, 2)
            tr1 = b.op("vector", lambda E, cx=cxa: E.tensor_scalar(
                out=rcpt[:, :], in0=cx[:, :, 128:129].squeeze(2),
                scalar1=EPS_ATTN, scalar2=None, op0=ALU.add), (tg,), "dv")
            b.op("vector", lambda E: E.reciprocal(
                out=rcpt[:, :], in_=rcpt[:, :]), (), "dv")
            tdv0 = b.op("vector", lambda E, cx=cxa, pp=p % 2: E.tensor_tensor(
                out=cxb[:, pp, 0:16, :], in0=cx[:, 0:16, 0:128],
                in1=rcpt[:, 0:16].unsqueeze(2).broadcast_to([128, 16, 128]),
                op=ALU.mult), (st_["cxb_free"][p % 2],), "dv")
            tdv1 = b.op("vector", lambda E, cx=cxa, pp=p % 2: E.tensor_tensor(
                out=cxb[:, pp, 16:31, :], in0=cx[:, 16:31, 0:128],
                in1=rcpt[:, 16:31].unsqueeze(2).broadcast_to([128, 15, 128]),
                op=ALU.mult), (), "dv")
            st_["cxa_free"][p % 2] = tdv1
            return (tdv0, tdv1)

        def e2_pair(p, tdv):
            ev2 = None
            for grp in range(2):
                us = list(range(1, 17)) if grp == 0 else list(range(17, 32))
                rb = grp
                mm = None
                for h2 in (0, 1):
                    hb = h2 * 64
                    b.pe_guard(hb, 64, hb)
                    for uu in us:
                        reads = ()
                        if uu == us[0] and h2 == 0:
                            reads = (tdv[grp], st_["pb"][2 * rb],
                                     st_["pb"][2 * rb + 1])
                        mm = b.op("tensor", lambda E, U=uu, h=hb, pp=p, r=rb:
                                  E.matmul(ps_b[h:h + 64, r,
                                                ((U - 1) % 16) * 64:((U - 1) % 16) * 64 + 64],
                                           cxb[h:h + 64, pp % 2, U - 1, h:h + 64],
                                           qfm[h:h + 64, pp, U * 64:U * 64 + 64],
                                           start=True, stop=True),
                                  reads, "pe")
                st_["e2_pe"] = mm
                st_["cxb_free"][p % 2] = mm
                n_u = len(us)
                ev2 = b.op("scalar", lambda E, g=grp, pp=p, r=rb, n=n_u:
                           E.activation(
                               out=at[:, pp, (16 * g + 1) * 64:(16 * g + 1 + n) * 64],
                               in_=ps_b[:, r, 0:n * 64], func=AF.Copy),
                           (mm, tok_at0), "ac")
                if grp == 0:
                    at_g0[p] = ev2
                st_["pb"][2 * rb] = ev2
                st_["pb"][2 * rb + 1] = ev2
            return ev2
        at_g0 = [None] * 8

        st_["cxa_free"] = [None, None]
        st_["cxb_free"] = [None, None]
        st_["e2_pe"] = None
        ev = e1_pair(0)
        tdv = chain_pair(0, ev)
        at_done = None
        for p in range(1, 8):
            ev = e1_pair(p)
            tdv_next = chain_pair(p, ev)
            at_done = e2_pair(p - 1, tdv)
            tdv = tdv_next
        b.phase('attn-outp')
        wTs = wTso
        twt = twtO
        ops_ = [ps_b[:, 0, :], ps_b[:, 1, :],
                ps_a[:, 0:2, :].rearrange("p a b -> p (a b)")]
        xout = [cxa2[:, pg, :, :].rearrange("p a b -> p (a b)")[:, 0:1024]
                for pg in range(2)]
        xout_tok = [st_["cxa_free"][0], st_["cxa_free"][1]]
        q3 = ("sync", "scalar", "gpsimd")
        ch0 = chunk_ap(x_dram, 0)
        tx0 = b.dma("sync", xst[:, 0, :], ch0,
                    reads=(st_["xst"][0], st_["x_load_extra"][0]))
        b.pe_guard(0, 128, 0)
        for jb in range(2):
            b.op("tensor", lambda E, j2=jb: E.matmul(
                ops_[2][:, j2 * 512:(j2 + 1) * 512], onesrow[0:1, :],
                bo_b[0:1, j2 * 512:(j2 + 1) * 512], start=True, stop=False),
                twt + (st_["pa"][0], st_["pa"][1], st_["pa"][2],
                       t_bo, t_or, tok_at0) if jb == 0 else (),
                None)
            for c in range(7):
                b.op("tensor", lambda E, c3=c, j2=jb, w=wTs:
                     E.matmul(ops_[2][:, j2 * 512:(j2 + 1) * 512],
                              at[:, c3, 0:128],
                              w[:, c3, j2 * 512:(j2 + 1) * 512],
                              start=False, stop=False),
                     (at_g0[c],) if jb == 0 else (), None)
        at_done = e2_pair(7, tdv)
        b.pe_guard(0, 128, 0)
        mm = None
        for jb in range(2):
            mm = b.op("tensor", lambda E, j2=jb, w=wTs:
                      E.matmul(ops_[2][:, j2 * 512:(j2 + 1) * 512],
                               at[:, 7, 0:128],
                               w[:, 7, j2 * 512:(j2 + 1) * 512],
                               start=False, stop=True),
                      (at_g0[7],) if jb == 0 else (),
                      "pe" if jb == 1 else None)
        tadd = b.op("vector", lambda E, xo=xout[0]: E.tensor_tensor(
            out=xo, in0=ops_[2][:, :], in1=xst[:, 0, :],
            op=ALU.add), (mm, tx0, xout_tok[0]), "dv")
        st_["xst"][0] = tadd
        tst0 = b.dma(q3[1], ch0, xout[0], reads=(tadd,))
        xout_tok[0] = tst0
        st_["prev_store"] = tst0
        st_["x_load_extra"][0] = tst0
        ops_tok = [(st_["pb"][0], st_["pb"][1]), (st_["pb"][2], st_["pb"][3]),
                   tadd]
        for tc in range(1, 16):
            sl = tc % 2
            ch = chunk_ap(x_dram, tc)
            tx = b.dma(q3[(2 * tc) % 3], xst[:, sl, :], ch,
                       reads=(st_["xst"][sl], st_["x_load_extra"][tc]))
            rb = (tc + 2) % 3
            mm = None
            for jb in range(2):
                b.op("tensor", lambda E, j2=jb, r=rb: E.matmul(
                    ops_[r][:, j2 * 512:(j2 + 1) * 512], onesrow[0:1, :],
                    bo_b[0:1, j2 * 512:(j2 + 1) * 512], start=True, stop=False),
                    twt + (at_done, ops_tok[rb], t_bo, t_or) if jb == 0 else (),
                    None)
                for c in range(8):
                    mm = b.op("tensor", lambda E, c3=c, j2=jb, t2=tc, r=rb, w=wTs:
                              E.matmul(ops_[r][:, j2 * 512:(j2 + 1) * 512],
                                       at[:, c3, t2 * 128:(t2 + 1) * 128],
                                       w[:, c3, j2 * 512:(j2 + 1) * 512],
                                       start=False, stop=(c3 == 7)),
                              (), "pe" if (jb == 1 and c == 7) else None)
            tadd = b.op("vector", lambda E, s=sl, r=rb, xo=xout[sl]:
                        E.tensor_tensor(
                            out=xo, in0=ops_[r][:, :], in1=xst[:, s, :],
                            op=ALU.add), (mm, tx, xout_tok[sl]), "dv")
            ops_tok[rb] = tadd
            st_["xst"][sl] = tadd
            tstore = b.dma(q3[(2 * tc + 1) % 3], ch, xout[sl], reads=(tadd,))
            xout_tok[sl] = tstore
            st_["prev_store"] = tstore
            st_["x_load_extra"][tc] = tstore
        st_["pb"][0] = ops_tok[0]
        st_["pb"][1] = ops_tok[0]
        st_["pb"][2] = ops_tok[1]
        st_["pb"][3] = ops_tok[1]
        st_["pa"][0] = ops_tok[2]
        st_["pa"][1] = ops_tok[2]
        stackE.close()
        stackP.close()
        st_["prev_l2_pe"] = st_["e2_pe"]

    st_["ln2_pre"] = [None] * 16

    def ln2_hook(gc, page, gate):
        if page is None:
            sl = gc % 2
            tx = b.dma("scalar", xst[:, sl, :], chunk_ap(x_dram, gc),
                       reads=(st_["xst"][sl], st_["x_load_extra"][gc]))
            pg = xst[:, sl, :]
            r = (tx,)
        else:
            pg = page
            r = gate
        b1 = b.op("vector", lambda E, p=pg: E.bn_stats(
            out=stat[:, 0, 0:6], in_=p[:, 0:512]), r)
        b2 = b.op("vector", lambda E, p=pg: E.bn_stats(
            out=stat[:, 0, 6:12], in_=p[:, 512:1024]), r)
        mrow = 1 if gc < 8 else 6
        mv = stat[:, mrow, 2 * (gc % 8):2 * (gc % 8) + 2]
        ag = b.op("vector", lambda E, m=mv: E.bn_aggr(
            out=m, in_=stat[:, 0, 0:12]), (b1, b2))
        tln = b.op("scalar", lambda E, m=mv, cc=gc: E.activation(
            out=stat[:, 7, cc:cc + 1], in_=m[:, 1:2],
            func=AF.Ln, bias=epsc[:, 0:1]), (ag, t_ep))
        trs = b.op("scalar", lambda E, cc=gc: E.activation(
            out=stat[:, 3, cc:cc + 1], in_=stat[:, 7, cc:cc + 1],
            func=AF.Exp, scale=-0.5), (tln,))
        if page is None:
            st_["xst"][gc % 2] = (b1, b2)
        st_["ln2_pre"][gc] = (mv, trs)

    n_ph = len(debug_phases)
    assert "ffn1" in debug_phases
    if n_ph > 1:
        st_["prev_xh_store"] = tuple(
            [(ds, b.cnt[ds], "_") for ds in b.dma_sems if b.cnt[ds]]
            + [("ac", b.cnt["ac"], "_")])
        st_["wprep_fn"] = wprep
    with ExitStack() as fstack:
        ffn_phase(x_in, x_dram if n_ph > 1 else out_d, fstack,
                  l2_hook=ln2_hook if "attn" in debug_phases else None)
    st_["ln2_stage_gate"] = ("dv", b.cnt["dv"], "_")
    if "attn" in debug_phases:
        b.fence()
        with ExitStack() as astack:
            attn_phase(astack)
    if "ffn2" in debug_phases:
        b.phase('ffn2')
        b.fence()
        with ExitStack() as fstack:
            ffn_phase(x_dram, out_d, fstack)
    elif n_ph > 1:
        for gc in range(16):
            sl = gc % 2
            tx = b.dma("sync", xst[:, sl, :], chunk_ap(x_dram, gc),
                       reads=(st_["xst"][sl], st_["prev_store"]))
            tst = b.dma("sync", chunk_ap(out_d, gc), xst[:, sl, :], reads=(tx,))
            st_["xst"][sl] = tst

    for ds in b.dma_sems:
        if b.cnt[ds]:
            b.streams["sync"].append(("w", ds, b.cnt[ds]))
    b.replay()
    return b


_PHASES = ("ffn1", "attn", "ffn2")


def kernel(L0_attn_x, L0_attn_Wq, L0_attn_Wk, L0_attn_Wv, L0_attn_Wo,
           L0_attn_bo, L0_ffn_k0, L0_ffn_k1, L0_ffn_k2,
           L0_ffn_s0, L0_ffn_s1, L0_ffn_s2):
    x = np.ascontiguousarray(np.asarray(L0_attn_x, dtype=np.float32))
    k0 = np.ascontiguousarray(np.asarray(L0_ffn_k0)).view(np.uint8)
    k1 = np.ascontiguousarray(np.asarray(L0_ffn_k1)).view(np.uint8)
    k2 = np.ascontiguousarray(np.asarray(L0_ffn_k2)).view(np.uint8)
    ws = {w: np.ascontiguousarray(np.asarray(a, dtype=np.float32))
          for w, a in zip("qkvo", (L0_attn_Wq, L0_attn_Wk, L0_attn_Wv, L0_attn_Wo))}
    bo = np.asarray(L0_attn_bo, dtype=np.float32).reshape(1, D)
    sv = np.ascontiguousarray(np.tile(
        np.stack([np.asarray(L0_ffn_s0, dtype=np.float32),
                  np.asarray(L0_ffn_s1, dtype=np.float32),
                  np.asarray(L0_ffn_s2, dtype=np.float32)]).reshape(1, 3),
        (128, 1)))
    ident = np.eye(128, dtype=np.float32).astype(ml_dtypes.bfloat16)

    b = build_program(_PHASES)
    in_maps = []
    for i in range(NCORES):
        in_maps.append(dict(
            x_in=x[i], k0=k0, k1=k1, k2=k2,
            Wq=ws["q"], Wk=ws["k"], Wv=ws["v"], Wo=ws["o"],
            bo=bo, sv=sv, ident=ident))
    res = run_bass_kernel_spmd(b.nc, in_maps, core_ids=list(range(NCORES)))
    return np.stack([r["out"] for r in res.results], axis=0)


# version 47
# speedup vs baseline: 1.0388x; 1.0044x over previous
import numpy as np
import ml_dtypes
from contextlib import ExitStack

import concourse.bass as bass
import concourse.mybir as mybir
from concourse.bass_utils import run_bass_kernel_spmd

F32 = mybir.dt.float32
BF16 = mybir.dt.bfloat16
U8 = mybir.dt.uint8
AF = mybir.ActivationFunctionType
ALU = mybir.AluOpType
AX = mybir.AxisListType

ENGINES = ("tensor", "vector", "scalar", "gpsimd", "sync")
SCALE_Q = float(64.0 ** (-0.25))
EPS_LN = 1e-5
EPS_ATTN = 1e-6
NCORES = 8
T = 2048
D = 1024
HID = 4096


class B:

    def __init__(self):
        import os
        self.nc = bass.Bass("TRN2", target_bir_lowering=False,
                            detect_race_conditions=bool(os.environ.get("RACEDET")))
        self.ctx = ExitStack()
        self.streams = {e: [] for e in ENGINES}
        self.cnt = {}
        self.semh = {}
        self.dma_sems = [f"q{qn}{i}" for qn in "sgav" for i in range(8)]
        for s in ("pe", "dv", "ac", "gp", "sy", *self.dma_sems):
            self.semh[s] = self.ctx.enter_context(self.nc.semaphore(s))
            self.cnt[s] = 0
        self.dmacnt = {"qs": 0, "qg": 0, "qa": 0, "qv": 0}
        self.tcount = 0
        self.phase_marks = []
        self.eng_sem = dict(tensor="pe", vector="dv", scalar="ac",
                            gpsimd="gp", sync="sy")

    _sbn = 0

    def sb(self, name, shape, dtype, stack=None):
        if name is None:
            B._sbn += 1
            name = f"sb{B._sbn}"
        return (stack or self.ctx).enter_context(
            self.nc.sbuf_tensor(name, shape, dtype))

    def ps(self, name, shape, dtype=F32):
        return self.ctx.enter_context(self.nc.psum_tensor(name, shape, dtype))

    def op(self, eng, fn, reads=(), inc=None, amt=1):
        if inc is None and eng in ("vector", "scalar", "gpsimd"):
            inc = self.eng_sem[eng]
        st = self.streams[eng]
        flat = []
        for tk in reads:
            if tk is None:
                continue
            if isinstance(tk, tuple) and len(tk) == 3 and isinstance(tk[0], str):
                flat.append(tk)
            else:
                flat.extend(t for t in tk if t is not None)
        for tk in flat:
            s, v, peng = tk
            if peng == eng == "tensor":
                continue
            assert v <= self.cnt[s], f"future token {s}={v} > {self.cnt[s]}"
            st.append(("w", s, v))
        if eng in ("vector", "scalar", "gpsimd"):
            own = self.eng_sem[eng]
            if self.cnt[own]:
                st.append(("w", own, self.cnt[own]))
        if eng == "tensor":
            self.tcount += 1
        tok = None
        if inc is not None:
            self.cnt[inc] += amt
            tok = (inc, self.cnt[inc], eng)
        st.append(("i", fn, inc, amt if inc else 0))
        return tok

    def dma(self, eng, out, in_, reads=(), transpose=False):
        assert eng in ("sync", "gpsimd", "scalar", "vector")
        q = {"sync": "qs", "gpsimd": "qg", "scalar": "qa", "vector": "qv"}[eng]
        n = self.dmacnt[q]
        self.dmacnt[q] += 1
        sem = q[1] and f"q{q[1]}{n % 8}"
        rounds = n // 8
        reads = tuple(reads)
        if rounds > 0:
            reads = reads + ((sem, 16 * rounds, "_"),)
        return self.op(eng, lambda E, o=out, i=in_, tr=transpose:
                       E.dma_start(out=o, in_=i, transpose=tr),
                       reads, sem, 16)

    _pe_base = (0,)

    def phase(self, name):
        self.phase_marks.append((name, self.tcount))

    def pe_guard(self, *cfg):
        if cfg != self._pe_base:
            if self.cnt["pe"]:
                self.streams["tensor"].append(("w", "pe", self.cnt["pe"]))
            self._pe_base = cfg

    def fence(self):
        toks = [(se, self.cnt[se], "_") for se in ("pe", "dv", "ac", "gp")
                if self.cnt[se]]
        for ds in self.dma_sems:
            if self.cnt[ds]:
                toks.append((ds, self.cnt[ds], "_"))
        for e in ENGINES:
            own = self.eng_sem[e]
            for tk in toks:
                if tk[0] == own:
                    continue
                self.streams[e].append(("w", tk[0], tk[1]))

    def replay(self):
        nc = self.nc
        with nc.Block() as blk:
            def mk(eng):
                def body(E):
                    seen = {}
                    nw = 0
                    for item in self.streams[eng]:
                        if item[0] == "w":
                            _, s, v = item
                            if seen.get(s, 0) >= v:
                                continue
                            seen[s] = v
                            if nw >= 2:
                                E.nop(nofuse=True)
                                nw = 0
                            E.wait_ge(self.semh[s], v)
                            nw += 1
                        else:
                            nw = 0
                            _, fn, inc, amt = item
                            inst = fn(E)
                            if inc is not None and amt:
                                inst.then_inc(self.semh[inc], amt)
                return body
            blk.tensor(mk("tensor"))
            blk.vector(mk("vector"))
            blk.scalar(mk("scalar"))
            blk.gpsimd(mk("gpsimd"))
            blk.sync(mk("sync"))


def chunk_ap(dram_ap, gc):
    return dram_ap.rearrange("(n p) d -> n p d", p=128)[gc]


def build_program(debug_phases=("ffn1", "attn", "ffn2")):
    b = B()
    nc = b.nc

    x_in = nc.dram_tensor("x_in", [T, D], F32, kind="ExternalInput").ap()
    k0_d = nc.dram_tensor("k0", [2 * D, HID], U8, kind="ExternalInput").ap()
    k1_d = nc.dram_tensor("k1", [HID, HID], U8, kind="ExternalInput").ap()
    k2_d = nc.dram_tensor("k2", [HID, D], U8, kind="ExternalInput").ap()
    w_d = {w: nc.dram_tensor("W" + w, [D, D], F32, kind="ExternalInput").ap()
           for w in ("q", "k", "v", "o")}
    bo_d = nc.dram_tensor("bo", [1, D], F32, kind="ExternalInput").ap()
    sv_d = nc.dram_tensor("sv", [128, 3], F32, kind="ExternalInput").ap()
    id_d = nc.dram_tensor("ident", [128, 128], BF16, kind="ExternalInput").ap()
    out_d = nc.dram_tensor("out", [T, D], F32, kind="ExternalOutput").ap()

    x_dram = nc.dram_tensor("x_scr", [T, D], F32, kind="Internal").ap()
    d0_dram = nc.dram_tensor("d0_scr", [D, HID], BF16, kind="Internal").ap()
    k1_pm = nc.dram_tensor("k1pm_scr", [HID, HID], BF16, kind="Internal").ap()
    k2_pm = nc.dram_tensor("k2pm_scr", [HID, D], BF16, kind="Internal").ap()
    wbf_dram = nc.dram_tensor("wbf_scr", [4 * D, D], BF16, kind="Internal").ap()

    idn = b.sb("idn", [128, 128], BF16)
    scol = b.sb("scol", [128, 3], F32)
    deriv = b.sb("deriv", [128, 3], F32)
    onescol = b.sb("onescol", [128, 1], BF16)
    onesrow = b.sb("onesrow", [1, 128], BF16)
    bo_b = b.sb("bo_b", [1, D], BF16)
    xst = b.sb("xst", [128, 2, D], F32)
    qb = b.sb("qb", [128, 2, D], BF16)
    stat = b.sb("stat", [128, 8, 16], F32)
    rcpt = b.sb("rcpt", [128, 31], F32)
    epsc = b.sb("epsc", [128, 1], F32)
    neg2 = b.sb("neg2", [128, 1], F32)

    ps_a = b.ps("ps_a", [128, 3, 512])
    ps_b = b.ps("ps_b", [128, 2, 1024])
    ps_t = b.ps("ps_t", [128, 8, 128], BF16)
    pbt = [ps_b[:, n // 2, (n % 2) * 512:(n % 2 + 1) * 512] for n in range(4)]
    tile7 = pbt + [ps_a[:, i, :] for i in range(3)]

    def t7_tok(i):
        return st_["pb"][i] if i < 4 else st_["pa"][i - 4]

    def t7_set(i, tok):
        if i < 4:
            st_["pb"][i] = tok
        else:
            st_["pa"][i - 4] = tok

    st_ = {
        "pa": [None, None, None],
        "pb": [None, None, None, None],
        "pt": None,
        "xst": [None, None],
        "qb": [None, None, None],
        "qfm_gate": None,
        "scr1": None, "scr2": None,
        "mv": [None, None],
        "q_pend": [],
        "wstq": [None, None, None, None],
        "rot7": 0,
        "x_load_extra": [None] * 16,
        "slot_stash": [None, None, None, None],
        "built": False,
        "t_w_stash": None,
        "prev_l2_pe": None,
        "prev_store": None,
        "prev_xh_store": None,
    }

    t_id = b.dma("sync", idn[:], id_d)
    t_bo = b.dma("gpsimd", bo_b[:], bo_d)
    t_oc = b.op("vector", lambda E: E.memset(onescol[:], 1.0), (), "dv")
    t_or = b.op("vector", lambda E: E.memset(onesrow[:], 1.0), (), "dv")
    t_ep = b.op("vector", lambda E: E.memset(epsc[:], EPS_LN), (), "dv")
    t_n2 = b.op("vector", lambda E: E.memset(neg2[:], -2.0), (), "dv")
    t_scol = b.dma("gpsimd", scol[:, :], sv_d)
    t_d0c = b.op("vector", lambda E: E.tensor_scalar(
        out=deriv[:, 0:1], in0=scol[:, 0:1], scalar1=4.0, scalar2=None,
        op0=ALU.mult), (t_scol,), "dv")
    t_d1c = b.op("vector", lambda E: E.tensor_scalar(
        out=deriv[:, 1:2], in0=scol[:, 1:2], scalar1=2.0, scalar2=None,
        op0=ALU.mult), (t_scol,), "dv")
    t_d2c = b.op("vector", lambda E: E.tensor_scalar(
        out=deriv[:, 2:3], in0=scol[:, 2:3], scalar1=0.5, scalar2=None,
        op0=ALU.mult), (t_scol,), "dv")

    t_wstash = {}

    def wprep():
        for wi, wname in enumerate(("q", "k", "v", "o")):
            t_wstash[wname] = b.dma(
                "gpsimd", wbf_dram[wi * D:(wi + 1) * D, :], w_d[wname],
                reads=())


    def ln_load(src_ap, gc, page, pg_tok, q):
        return b.dma(q, page, chunk_ap(src_ap, gc),
                     reads=(pg_tok, st_["x_load_extra"][gc]))

    def ln_stats(src_ap, gc, c, page=None, tx_pre=None):
        sl = gc % 2
        ch = chunk_ap(src_ap, gc)
        if page is None:
            pg, is_xst = xst[:, sl, :], True
            gates = (st_["xst"][sl], st_["x_load_extra"][gc])
            tx_a = b.dma("sync", pg[:, 0:512], ch[:, 0:512], reads=gates)
            tx_b = b.dma("gpsimd", pg[:, 512:1024], ch[:, 512:1024],
                         reads=gates)
            tx = (tx_a, tx_b)
            r1, r2 = (tx_a,), (tx_b,)
        else:
            pg, is_xst = page, False
            tx = tx_pre
            r1 = r2 = (tx,)
        b1 = b.op("vector", lambda E, p=pg: E.bn_stats(
            out=stat[:, 0, 0:6], in_=p[:, 0:512]), r1)
        b2 = b.op("vector", lambda E, p=pg: E.bn_stats(
            out=stat[:, 0, 6:12], in_=p[:, 512:1024]), r2)
        mvsl = c % 2
        mv = stat[:, 1, 2 * mvsl:2 * mvsl + 2]
        ag = b.op("vector", lambda E, m=mv: E.bn_aggr(
            out=m, in_=stat[:, 0, 0:12]), (b1, b2, st_["mv"][mvsl]))
        tln = b.op("scalar", lambda E, m=mv, cc=c: E.activation(
            out=stat[:, 7, cc:cc + 1], in_=m[:, 1:2],
            func=AF.Ln, bias=epsc[:, 0:1]), (ag, t_ep))
        trs = b.op("scalar", lambda E, cc=c: E.activation(
            out=stat[:, 3, cc:cc + 1], in_=stat[:, 7, cc:cc + 1],
            func=AF.Exp, scale=-0.5), (tln,))
        return (sl, c, mv, trs, tx, pg, is_xst)

    def ln_apply(sl, c, mv, trs, tx, pg, is_xst, dst_fn, extra_xn_reads,
                 eng="vector"):
        dst = dst_fn(sl)
        txn = b.op(eng, lambda E, p=pg, cc=c, d=dst, m=mv: E.tensor_scalar(
            out=d, in0=p,
            scalar1=m[:, 0:1], scalar2=stat[:, 3, cc:cc + 1],
            op0=ALU.subtract, op1=ALU.mult),
            (trs, tx) + tuple(extra_xn_reads))
        st_["mv"][c % 2] = (txn,)
        if is_xst:
            st_["xst"][sl] = (txn,)
        return txn, sl

    def transpose8(src_getter, dst_ap, reads, eng="vector"):
        ttr = None
        for c2 in range(8):
            ttr = b.op("tensor", lambda E, c3=c2: E.transpose(
                ps_t[:, c3, :], src_getter(c3), idn[:]),
                tuple(reads) + (st_["pt"], t_id), "pe")
        if eng == "scalar":
            tev = b.op("scalar", lambda E, d=dst_ap: E.activation(
                out=d, in_=ps_t[:, :, :], func=AF.Copy), (ttr,), "ac")
        else:
            tev = b.op("vector", lambda E, d=dst_ap: E.tensor_copy(
                out=d, in_=ps_t[:, :, :]), (ttr,), "dv")
        st_["pt"] = tev
        return tev

    def ffn_phase(src_ap, dst_ap, fstack, l2_hook=None):
        b.pe_guard(0, 128, 0)
        a1 = b.sb(None, [128, 32, 1024], BF16, fstack)
        a2 = b.sb(None, [128, 32, 1024], BF16, fstack)
        wst = b.sb(None, [128, 4, 8, 512], BF16, fstack)
        u8a = b.sb(None, [128, 8, 512], U8, fstack)
        u8b = b.sb(None, [128, 8, 512], U8, fstack)
        xnf = b.sb(None, [128, 8, 1024], BF16, fstack)
        xh = a1[:, 0:16, :].bitcast(F32).rearrange("p (a c) n -> p a (c n)", c=2)
        a2x = a2[:, 0:16, :].bitcast(F32).rearrange("p (a c) n -> p a (c n)", c=2)

        u8_tok = [None, None]
        a1_evac = None
        a2_evac = None
        l1_done = None
        build = not st_["built"]
        l0_pe = [None, None]

        def wslot_load(qslot, view, u8_src, pm_blk, pm_src, first, sc_col):
            if first:
                tq = b.dma("gpsimd", view, u8_src,
                           reads=(st_["wstq"][qslot], st_["slot_stash"][qslot]))
                tcv = b.op("vector", lambda E, v=view: E.tensor_scalar(
                    out=v, in0=v, scalar1=2.0, scalar2=-1.0,
                    op0=ALU.mult, op1=ALU.add), (tq,), "dv")
                tst = b.dma("sync", pm_blk, view, reads=(tcv,))
                st_["t_w_stash"] = tst
                st_["slot_stash"][qslot] = tst
                return (tcv,), tst
            else:
                tq = b.dma("gpsimd", view, pm_src,
                           reads=(st_["wstq"][qslot], st_["slot_stash"][qslot],
                                  st_["t_w_stash"]))
                return (tq,), None

        def ln_half_steps(half):
            toks = []
            state = {"pend": None, "c": 0}

            def flush():
                sl, c, mv, trs, tx, pg, isx = state["pend"]
                txn, _ = ln_apply(sl, c, mv, trs, tx, pg, isx,
                                  lambda s: qb[:, s, :], (st_["qb"][sl],))
                st_["qb"][sl] = txn
                t_last = transpose8(
                    lambda c3, s=sl: qb[:, s, c3 * 128:(c3 + 1) * 128],
                    xnf[:, :, c * 128:(c + 1) * 128],
                    (txn, l0_pe[1 - half]))
                st_["qb"][sl] = t_last
                toks.append(t_last)
                state["pend"] = None

            def step():
                c = state["c"]
                if c < 8:
                    if half == 0:
                        if c == 0:
                            state["txs"] = [
                                ln_load(src_ap, cc, a2x[:, cc, :], None,
                                        ("sync", "scalar", "gpsimd")[cc % 3])
                                for cc in range(8)]
                        pend_new = ln_stats(src_ap, c, c, page=a2x[:, c, :],
                                            tx_pre=state["txs"][c])
                    else:
                        pend_new = ln_stats(src_ap, 8 + c, c)
                    if state["pend"] is not None:
                        flush()
                    state["pend"] = pend_new
                elif state["pend"] is not None:
                    flush()
                state["c"] = c + 1
            return step, toks

        def ln_half(half):
            step, toks = ln_half_steps(half)
            for _ in range(9):
                step()
            return toks

        def l0_half(half, t_xn_last):
            rp = 0
            nonlocal a1_evac
            for nb in range(8):
                sl = nb % 2
                dsl = wst[:, sl, 0:8, :]
                d0blk = d0_dram[:, nb * 512:(nb + 1) * 512].rearrange(
                    "(cc p) n -> p cc n", p=128)
                if build and half == 0:
                    ta = b.dma("sync", u8a[:],
                               k0_d[0:D, nb * 512:(nb + 1) * 512]
                               .rearrange("(cc p) n -> p cc n", p=128),
                               reads=(u8_tok[0],))
                    tb = b.dma("sync", u8b[:],
                               k0_d[D:2 * D, nb * 512:(nb + 1) * 512]
                               .rearrange("(cc p) n -> p cc n", p=128),
                               reads=(u8_tok[1],))
                    td1 = b.op("vector", lambda E, ds=dsl: E.tensor_tensor(
                        out=ds, in0=u8a[:], in1=u8b[:], op=ALU.subtract),
                        (ta, tb, st_["wstq"][sl], st_["slot_stash"][sl], t_d0c),
                        "dv")
                    u8_tok[0] = td1
                    u8_tok[1] = td1
                    tslot = b.op("vector", lambda E, ds=dsl: E.tensor_scalar(
                        out=ds, in0=ds, scalar1=deriv[:, 0:1], scalar2=None,
                        op0=ALU.mult), (), "dv")
                    st_["t_w_stash"] = b.dma("sync", d0blk, dsl, reads=(tslot,))
                    st_["slot_stash"][sl] = st_["t_w_stash"]
                    tslot = (tslot, st_["t_w_stash"])
                else:
                    tslot = b.dma("sync", dsl, d0blk,
                                  reads=(st_["wstq"][sl], st_["slot_stash"][sl],
                                         st_["t_w_stash"]))
                    tslot = (tslot,)
                grp_last = None
                for n in range(4):
                    for th in range(2):
                        rp = st_["rot7"]
                        st_["rot7"] = (rp + 1) % 7
                        reads = tslot + (t_xn_last[th * 4 + 3], t7_tok(rp))
                        mm = None
                        for cch in range(8):
                            mm = b.op("tensor",
                                      lambda E, r=rp, nn=n, t2=th, c3=cch, ds=dsl:
                                      E.matmul(tile7[r],
                                               ds[:, c3, nn * 128:(nn + 1) * 128],
                                               xnf[:, c3, t2 * 512:(t2 + 1) * 512],
                                               start=(c3 == 0), stop=(c3 == 7)),
                                      reads if cch == 0 else (),
                                      "pe" if cch == 7 else None)
                        grp_last = mm
                        ev_reads = [mm]
                        if nb == 0 and n == 0 and th == 0:
                            ev_reads.append(st_["prev_xh_store"])
                        tev = b.op("scalar", lambda E, r=rp, nn=n, t2=th, nb2=nb:
                                   E.activation(out=a1[:, nb2 * 4 + nn,
                                                       t2 * 512:(t2 + 1) * 512],
                                                in_=tile7[r], func=AF.Relu),
                                   ev_reads, "ac")
                        t7_set(rp, tev)
                        a1_evac = tev
                st_["wstq"][sl] = grp_last
            l0_pe[half] = grp_last

        b.phase(f'ffn-ln0')
        t_xn = ln_half(0)
        b.phase(f'ffn-l0a')
        l0_half(0, t_xn)
        b.phase('ffn-ln1')
        ln1_step, t_xn_next = ln_half_steps(1)
        for half in range(2):
            if half == 1:
                for _ in range(9):
                    ln1_step()
                l0_half(1, t_xn_next)
            b.phase(f'ffn-l1-{half}')
            if half == 0 and build and st_.get("wprep_fn") is not None:
                st_["wprep_fn"]()
                st_["wprep_fn"] = None
            for nb in range(8):
                if half == 0:
                    ln1_step()
                qtok = [None] * 4
                for q in range(4):
                    u8src = k1_d[q * 1024:(q + 1) * 1024,
                                 nb * 512:(nb + 1) * 512].rearrange(
                        "(cc p) n -> p cc n", p=128)
                    pmblk = k1_pm[q * 1024:(q + 1) * 1024,
                                  nb * 512:(nb + 1) * 512].rearrange(
                        "(cc p) n -> p cc n", p=128)
                    qtok[q], _ = wslot_load(q, wst[:, q, :, :], u8src,
                                            pmblk, pmblk,
                                            build and half == 0, None)
                for th in range(2):
                    tl = [0, 0, 0, 0]
                    mm_last = [None] * 4
                    for n in range(4):
                        tl[n] = st_["rot7"]
                        st_["rot7"] = (st_["rot7"] + 1) % 7
                    for c in range(32):
                        q = c // 8
                        for n in range(4):
                            qmark = (th == 1 and c % 8 == 7 and n == 3)
                            reads = []
                            if n == 0 and c % 8 == 0:
                                reads.append(qtok[q])
                            if c == 0:
                                reads += [t7_tok(tl[n]), a1_evac]
                            mm = b.op("tensor",
                                      lambda E, nn=n, c3=c, t2=th, qq=q, r=tl[n]:
                                      E.matmul(tile7[r],
                                               wst[:, qq, c3 % 8, nn * 128:(nn + 1) * 128],
                                               a1[:, c3, t2 * 512:(t2 + 1) * 512],
                                               start=(c3 == 0), stop=(c3 == 31)),
                                      reads, "pe" if (c == 31 or qmark) else None)
                            if c == 31:
                                mm_last[n] = mm
                            if qmark:
                                st_["wstq"][q] = mm
                    for n in range(4):
                        tev = b.op("scalar", lambda E, nn=n, t2=th, nb2=nb, r=tl[n]:
                                   E.activation(out=a2[:, nb2 * 4 + nn,
                                                       t2 * 512:(t2 + 1) * 512],
                                                in_=tile7[r], func=AF.Relu,
                                                scale=deriv[:, 1:2]),
                                   (mm_last[n], t_d1c), "ac")
                        t7_set(tl[n], tev)
                        a2_evac = tev
            l1_done = st_["wstq"][3]
            if half == 0:
                ln1_step()

            b.phase(f'ffn-l2-{half}')
            xh_tok = [None] * 8
            for t128 in range(8):
                gc = half * 8 + t128
                xh_tok[t128] = b.dma("sync", xh[:, t128, :], chunk_ap(src_ap, gc),
                                     reads=(l1_done, st_["prev_xh_store"],
                                            st_["prev_store"]))
            if half == 1 and l2_hook is not None:
                for gc in range(8):
                    l2_hook(gc, None, None)
            evac_tok = [None] * 8
            h_stores = []
            for nb in range(2):
                qtok2 = [None] * 4
                for q in range(4):
                    u8src = k2_d[q * 1024:(q + 1) * 1024,
                                 nb * 512:(nb + 1) * 512].rearrange(
                        "(cc p) n -> p cc n", p=128)
                    pmblk = k2_pm[q * 1024:(q + 1) * 1024,
                                  nb * 512:(nb + 1) * 512].rearrange(
                        "(cc p) n -> p cc n", p=128)
                    qtok2[q], _ = wslot_load(q, wst[:, q, :, :], u8src,
                                             pmblk, pmblk,
                                             build and half == 0, None)
                for t128 in range(8):
                    r3 = st_["rot7"]
                    st_["rot7"] = (r3 + 1) % 7
                    mm = None
                    for c in range(32):
                        q = c // 8
                        reads = []
                        if c % 8 == 0:
                            reads.append(qtok2[q])
                        if c == 0:
                            reads += [a2_evac, t7_tok(r3)]
                        qmark2 = (c % 8 == 7 and t128 == 7)
                        mm = b.op("tensor",
                                  lambda E, c3=c, tt=t128, qq=q, r=r3:
                                  E.matmul(tile7[r],
                                           a2[:, c3, tt * 128:(tt + 1) * 128],
                                           wst[:, qq, c3 % 8, :],
                                           start=(c3 == 0), stop=(c3 == 31)),
                                  reads, "pe" if (c == 31 or qmark2) else None)
                        if qmark2:
                            st_["wstq"][q] = mm
                    tst = b.op("vector", lambda E, tt=t128, r=r3, nb2=nb:
                               E.scalar_tensor_tensor(
                                   out=xh[:, tt, nb2 * 512:(nb2 + 1) * 512],
                                   in0=tile7[r], scalar=deriv[:, 2:3],
                                   in1=xh[:, tt, nb2 * 512:(nb2 + 1) * 512],
                                   op0=ALU.mult, op1=ALU.add),
                               (mm, t_d2c, xh_tok[t128]), "dv")
                    t7_set(r3, tst)
                    if nb == 1:
                        tstore = b.dma("sync", chunk_ap(dst_ap, half * 8 + t128),
                                       xh[:, t128, :], reads=(tst,))
                        h_stores.append(tstore)
                        st_["prev_store"] = tstore
                        st_["x_load_extra"][half * 8 + t128] = tstore
                        if half == 1 and l2_hook is not None:
                            l2_hook(8 + t128, xh[:, t128, :], (tst,))
            st_["prev_l2_pe"] = mm
            st_["prev_xh_store"] = tuple(h_stores)
        st_["built"] = True

    def attn_phase(astack):
        qfm = b.sb(None, [128, 8, 2048], BF16, astack)
        kk = b.sb(None, [128, 16, 16, 64], BF16, astack)
        vv = b.sb(None, [128, 16, 8, 130], BF16, astack)
        at = b.sb(None, [128, 8, 2048], BF16, astack)
        xn2fm = at

        stackP = ExitStack()
        wTsA = b.sb(None, [128, 8, 1024], BF16, stackP)
        stackP2 = ExitStack()
        wTsB = b.sb(None, [128, 8, 1024], BF16, stackP2)
        scr1 = b.sb("scr1", [128, D], F32, stackP2)

        def load_wT(buf, wi, reads):
            return tuple(
                b.dma("sync", buf[:, cc, :],
                      wbf_dram[wi * D:(wi + 1) * D, cc * 128:(cc + 1) * 128],
                      reads=reads, transpose=True)
                for cc in range(8))

        b.phase('attn-ln2')
        qfmx = qfm.bitcast(F32)
        kkxs = kk.bitcast(F32)

        def ln2_page(gc):
            if gc < 8:
                return qfmx[:, gc, :]
            j = gc - 8
            return kkxs[:, 2 * j:2 * j + 2, :, :].rearrange(
                "p a b c -> p (a b c)")

        ln2_tx = [ln_load(x_dram, gc, ln2_page(gc),
                          st_["ln2_pre"][8 + gc][2] if gc < 8
                          else st_["ln2_stage_gate"],
                          "scalar" if gc < 8 else "gpsimd")
                  for gc in range(16)]
        stage_tok = [None] * 16
        xn2_tok = [None] * 16
        twt_cur = None
        soft_pe = ("pe", b.cnt["pe"], "_")
        for gc in range(16):
            if gc == 8:
                twt_cur = load_wT(wTsA, 0, (t_wstash["q"],))
            mv, trs = st_["ln2_pre"][gc][:2]
            sl = gc % 2
            txn, _ = ln_apply(sl, gc, mv, trs, ln2_tx[gc], ln2_page(gc),
                              False, lambda s: qb[:, s, :], (st_["qb"][sl],))
            stage_tok[gc] = txn
            st_["qb"][sl] = txn
            t_xn_last = transpose8(
                lambda c3, s=sl: qb[:, s, c3 * 128:(c3 + 1) * 128],
                xn2fm[:, :, gc * 128:(gc + 1) * 128], (txn, soft_pe))
            st_["qb"][sl] = t_xn_last
            xn2_tok[gc] = t_xn_last
        st_["qfm_gate"] = ("dv", b.cnt["dv"], "_")
        st_["qb"][2] = stage_tok[15]

        kk_tok = [None] * 16
        vv_tok = [None] * 16
        t_vones = b.op("vector", lambda E: E.memset(vv[:, :, :, 128:130], 1.0),
                       (("pe", b.cnt["pe"], "_"),), "dv")
        b.phase('attn-proj')
        def qpage(sl):
            if sl < 2:
                return qb[:, sl, :]
            return kk[:, 15, :, :].rearrange("p h e -> p (h e)")

        def q_flush():
            psl_, ptc_, pt7_ = st_["q_pend"].pop(0)
            extra = ()
            if st_["qfm_gate"] is not None:
                extra = (st_["qfm_gate"],)
                st_["qfm_gate"] = None
            tev = transpose8(
                lambda c3, s=psl_: qpage(s)[:, c3 * 128:(c3 + 1) * 128],
                qfm[:, :, ptc_ * 128:(ptc_ + 1) * 128], (pt7_,) + extra)
            st_["qb"][psl_] = tev

        qps = [ps_b[:, 0, :], ps_b[:, 1, :], ps_a[:, 0:2, :].rearrange("p a b -> p (a b)")]
        qps_tok = [(st_["pb"][0], st_["pb"][1]),
                   (st_["pb"][2], st_["pb"][3]),
                   (st_["pa"][0], st_["pa"][1])]
        proj_pe_last = None
        for wname in ("q", "k", "v"):
            if wname == "q":
                wTs, twt = wTsA, twt_cur
                twt_next = load_wT(wTsB, 1, (t_wstash["k"],))
            elif wname == "k":
                wTs, twt = wTsB, twt_next
                twt_next = load_wT(wTsA, 2,
                                   (t_wstash["v"],
                                    ("pe", b.cnt["pe"], "_")))
            else:
                wTs, twt = wTsA, twt_next
            for tc in range(16):
                rb = tc % 3
                mm = None
                for jb in range(2):
                    for c in range(8):
                        reads = ()
                        if jb == 0 and c == 0:
                            reads = twt + (xn2_tok[t2 if False else tc], qps_tok[rb])
                        mm = b.op("tensor", lambda E, c3=c, j2=jb, t2=tc, r=rb, w=wTs:
                                  E.matmul(qps[r][:, j2 * 512:(j2 + 1) * 512],
                                           xn2fm[:, c3, t2 * 128:(t2 + 1) * 128],
                                           w[:, c3, j2 * 512:(j2 + 1) * 512],
                                           start=(c3 == 0), stop=(c3 == 7)),
                                  reads, "pe" if (jb == 1 and c == 7) else None)
                proj_pe_last = mm
                if wname == "q":
                    t4 = b.op("scalar", lambda E, r=rb: E.activation(
                        out=scr1[:], in_=qps[r][:, :], func=AF.Exp,
                        scale=SCALE_Q), (mm, st_["scr1"], t_n2), "ac")
                    qps_tok[rb] = t4
                    t5 = b.op("vector", lambda E: E.reduce_sum(
                        out=stat[:, 5, :].unsqueeze(2),
                        in_=scr1[:, :].rearrange("p (h e) -> p h e", e=64),
                        axis=AX.X), (t4,), "dv")
                    b.op("vector", lambda E: E.reciprocal(
                        out=stat[:, 5, :], in_=stat[:, 5, :]), (), "dv")
                    sl = tc % 3
                    t7 = b.op("vector", lambda E, s=sl: E.tensor_tensor(
                        out=qpage(s).rearrange("p (h e) -> p h e", e=64),
                        in0=scr1[:, :].rearrange("p (h e) -> p h e", e=64),
                        in1=stat[:, 5, :].unsqueeze(2).broadcast_to([128, 16, 64]),
                        op=ALU.mult), (st_["qb"][sl],), "dv")
                    st_["scr1"] = t7
                    st_["q_pend"].append((sl, tc, t7))
                    if len(st_["q_pend"]) == 3:
                        q_flush()
                elif wname == "k":
                    while st_["q_pend"]:
                        q_flush()
                    sl = tc % 2
                    tmx = b.op("vector", lambda E, r=rb, s=sl: E.tensor_scalar(
                        out=qb[:, s, :], in0=qps[r][:, :], scalar1=SCALE_Q,
                        scalar2=0.0, op0=ALU.mult, op1=ALU.max),
                        (mm, st_["qb"][sl]), "dv")
                    te = b.op("scalar", lambda E, r=rb: E.activation(
                        out=scr1[:], in_=qps[r][:, :], func=AF.Exp,
                        scale=SCALE_Q), (tmx, st_["scr1"]), "ac")
                    qps_tok[rb] = te
                    t4_reads = (te, tmx, stage_tok[8 + tc // 2])
                    if tc == 15:
                        t4_reads = t4_reads + (st_["qb"][2],)
                    t4 = b.op("vector", lambda E, t2=tc, s=sl: E.scalar_tensor_tensor(
                        out=kk[:, t2, :, :].rearrange("p h e -> p (h e)"),
                        in0=scr1[:], scalar=1.0, in1=qb[:, s, :],
                        op0=ALU.min, op1=ALU.add), t4_reads, "dv")
                    st_["scr1"] = t4
                    st_["qb"][sl] = t4
                    kk_tok[tc] = t4
                else:
                    tv = b.op("scalar", lambda E, t2=tc, r=rb: E.activation(
                        out=vv[:, t2, :, 0:128], in_=qps[r][:, :]
                        .rearrange("p (h e) -> p h e", e=128),
                        func=AF.Copy), (mm,), "ac")
                    qps_tok[rb] = tv
                    vv_tok[tc] = tv
        st_["pb"][0] = qps_tok[0]
        st_["pb"][1] = qps_tok[0]
        st_["pb"][2] = qps_tok[1]
        st_["pb"][3] = qps_tok[1]
        st_["pa"][0] = qps_tok[2]
        st_["pa"][1] = qps_tok[2]
        tok_vdone = ("pe", b.cnt["pe"], "_")
        stackP2.close()

        tok_proj = ("dv", b.cnt["dv"], "_")
        tok_at0 = b.op("vector", lambda E: E.memset(at[:, :, 0:64], 0.0),
                       (("pe", b.cnt["pe"], "_"),), "dv")

        stackE = ExitStack()
        cxa2 = b.sb(None, [128, 2, 31, 129], F32, stackE)
        cxb = b.sb(None, [128, 2, 31, 128], BF16, stackE)
        wTso = wTsA
        twtO = load_wT(wTsA, 3, (t_wstash["o"], tok_vdone))

        def e1_pair(p):
            cxa = cxa2[:, p % 2]
            order = list(range(0, 31, 2)) + list(range(1, 31, 2))
            evs = []
            pos = 0
            while pos < 31:
                nwave = min(9, 31 - pos)
                if pos < 16 and pos + nwave > 16:
                    nwave = 16 - pos
                wave = order[pos:pos + nwave]
                mm2 = None
                tcb_max = max(wave) // 2
                hb = (wave[0] % 2) * 64
                b.pe_guard(hb, 128, 0)
                for i, uu in enumerate(wave):
                    bank = i // 3
                    slot = i % 3
                    tcb = uu // 2
                    reads = ()
                    if i == 0:
                        reads = (kk_tok[tcb_max], vv_tok[tcb_max], t_vones,
                                 st_["pa"][0], st_["pa"][1],
                                 st_["pa"][2], st_["cxa_free"][p % 2])
                    mm2 = b.op("tensor", lambda E, bk=bank, so=slot, tb=tcb, h=hb, pp=p:
                         E.matmul(ps_a[:, bk, so * 136:so * 136 + 129],
                                  kk[h:h + 64, tb, 2 * pp:2 * pp + 2, :]
                                  .rearrange("p h e -> p (h e)"),
                                  vv[h:h + 64, tb, pp, 0:129],
                                  start=True, stop=True), reads, "pe")
                nb_banks = (nwave + 2) // 3
                for bk in range(nb_banks):
                    nin = min(3, nwave - 3 * bk)
                    U0 = wave[3 * bk]
                    ev_reads = (mm2,)
                    if p == 0 and pos == 0 and bk == 0:
                        ev_reads = (mm2, tok_vdone)
                    tevw = b.op("scalar", lambda E, u0=U0, nn=nin, bkk=bk,
                                cx=cxa:
                                E.activation(
                                    out=cx[:, u0:u0 + 2 * nn - 1:2, :],
                                    in_=ps_a[:, bkk, 0:136 * nn]
                                    .rearrange("p (s e) -> p s e", e=136)[:, :, 0:129],
                                    func=AF.Copy),
                                ev_reads, "ac")
                    st_["pa"][bk] = tevw
                    evs.append(tevw)
                pos += nwave
            return evs[-1]

        def chain_pair(p, ev_tok):
            cxa = cxa2[:, p % 2]

            def sadd(dst, src, n, step, reads=()):
                return b.op("vector", lambda E, cx=cxa, a=dst, bb=src, nn=n,
                            s=step:
                            E.tensor_tensor(
                                out=cx[:, a:a + (nn - 1) * s + 1:s, :],
                                in0=cx[:, a:a + (nn - 1) * s + 1:s, :],
                                in1=cx[:, bb:bb + (nn - 1) * s + 1:s, :],
                                op=ALU.add), reads, "dv")
            sadd(1, 0, 15, 2, (ev_tok,))
            sadd(3, 1, 7, 4)
            sadd(7, 3, 3, 8)
            sadd(15, 7, 1, 16)
            sadd(23, 15, 1, 16)
            sadd(11, 7, 3, 8)
            sadd(5, 3, 7, 4)
            tg = sadd(2, 1, 15, 2)
            tr1 = b.op("vector", lambda E, cx=cxa: E.tensor_scalar(
                out=rcpt[:, :], in0=cx[:, :, 128:129].squeeze(2),
                scalar1=EPS_ATTN, scalar2=None, op0=ALU.add), (tg,), "dv")
            b.op("vector", lambda E: E.reciprocal(
                out=rcpt[:, :], in_=rcpt[:, :]), (), "dv")
            tdv0 = b.op("vector", lambda E, cx=cxa, pp=p % 2: E.tensor_tensor(
                out=cxb[:, pp, 0:16, :], in0=cx[:, 0:16, 0:128],
                in1=rcpt[:, 0:16].unsqueeze(2).broadcast_to([128, 16, 128]),
                op=ALU.mult), (st_["cxb_free"][p % 2],), "dv")
            tdv1 = b.op("vector", lambda E, cx=cxa, pp=p % 2: E.tensor_tensor(
                out=cxb[:, pp, 16:31, :], in0=cx[:, 16:31, 0:128],
                in1=rcpt[:, 16:31].unsqueeze(2).broadcast_to([128, 15, 128]),
                op=ALU.mult), (), "dv")
            st_["cxa_free"][p % 2] = tdv1
            return (tdv0, tdv1)

        def e2_pair(p, tdv):
            ev2 = None
            for grp in range(2):
                us = list(range(1, 17)) if grp == 0 else list(range(17, 32))
                rb = grp
                mm = None
                for h2 in (0, 1):
                    hb = h2 * 64
                    b.pe_guard(hb, 64, hb)
                    for uu in us:
                        reads = ()
                        if uu == us[0] and h2 == 0:
                            reads = (tdv[grp], st_["pb"][2 * rb],
                                     st_["pb"][2 * rb + 1])
                        mm = b.op("tensor", lambda E, U=uu, h=hb, pp=p, r=rb:
                                  E.matmul(ps_b[h:h + 64, r,
                                                ((U - 1) % 16) * 64:((U - 1) % 16) * 64 + 64],
                                           cxb[h:h + 64, pp % 2, U - 1, h:h + 64],
                                           qfm[h:h + 64, pp, U * 64:U * 64 + 64],
                                           start=True, stop=True),
                                  reads, "pe")
                st_["e2_pe"] = mm
                st_["cxb_free"][p % 2] = mm
                n_u = len(us)
                ev2 = b.op("scalar", lambda E, g=grp, pp=p, r=rb, n=n_u:
                           E.activation(
                               out=at[:, pp, (16 * g + 1) * 64:(16 * g + 1 + n) * 64],
                               in_=ps_b[:, r, 0:n * 64], func=AF.Copy),
                           (mm, tok_at0), "ac")
                if grp == 0:
                    at_g0[p] = ev2
                st_["pb"][2 * rb] = ev2
                st_["pb"][2 * rb + 1] = ev2
            return ev2
        at_g0 = [None] * 8

        st_["cxa_free"] = [None, None]
        st_["cxb_free"] = [None, None]
        st_["e2_pe"] = None
        ev = e1_pair(0)
        tdv = chain_pair(0, ev)
        at_done = None
        for p in range(1, 8):
            ev = e1_pair(p)
            tdv_next = chain_pair(p, ev)
            at_done = e2_pair(p - 1, tdv)
            tdv = tdv_next
        b.phase('attn-outp')
        wTs = wTso
        twt = twtO
        ops_ = [ps_b[:, 0, :], ps_b[:, 1, :],
                ps_a[:, 0:2, :].rearrange("p a b -> p (a b)")]
        xout = [cxa2[:, pg, :, :].rearrange("p a b -> p (a b)")[:, 0:1024]
                for pg in range(2)]
        xout_tok = [st_["cxa_free"][0], st_["cxa_free"][1]]
        q3 = ("sync", "scalar", "gpsimd")
        ch0 = chunk_ap(x_dram, 0)
        tx0 = b.dma("sync", xst[:, 0, :], ch0,
                    reads=(st_["xst"][0], st_["x_load_extra"][0]))
        b.pe_guard(0, 128, 0)
        for jb in range(2):
            b.op("tensor", lambda E, j2=jb: E.matmul(
                ops_[2][:, j2 * 512:(j2 + 1) * 512], onesrow[0:1, :],
                bo_b[0:1, j2 * 512:(j2 + 1) * 512], start=True, stop=False),
                twt + (st_["pa"][0], st_["pa"][1], st_["pa"][2],
                       t_bo, t_or, tok_at0) if jb == 0 else (),
                None)
            for c in range(7):
                b.op("tensor", lambda E, c3=c, j2=jb, w=wTs:
                     E.matmul(ops_[2][:, j2 * 512:(j2 + 1) * 512],
                              at[:, c3, 0:128],
                              w[:, c3, j2 * 512:(j2 + 1) * 512],
                              start=False, stop=False),
                     (at_g0[c],) if jb == 0 else (), None)
        at_done = e2_pair(7, tdv)
        b.pe_guard(0, 128, 0)
        mm = None
        for jb in range(2):
            mm = b.op("tensor", lambda E, j2=jb, w=wTs:
                      E.matmul(ops_[2][:, j2 * 512:(j2 + 1) * 512],
                               at[:, 7, 0:128],
                               w[:, 7, j2 * 512:(j2 + 1) * 512],
                               start=False, stop=True),
                      (at_g0[7],) if jb == 0 else (),
                      "pe" if jb == 1 else None)
        tadd = b.op("vector", lambda E, xo=xout[0]: E.tensor_tensor(
            out=xo, in0=ops_[2][:, :], in1=xst[:, 0, :],
            op=ALU.add), (mm, tx0, xout_tok[0]), "dv")
        st_["xst"][0] = tadd
        tst0 = b.dma(q3[1], ch0, xout[0], reads=(tadd,))
        xout_tok[0] = tst0
        st_["prev_store"] = tst0
        st_["x_load_extra"][0] = tst0
        ops_tok = [(st_["pb"][0], st_["pb"][1]), (st_["pb"][2], st_["pb"][3]),
                   tadd]
        for tc in range(1, 16):
            sl = tc % 2
            ch = chunk_ap(x_dram, tc)
            tx = b.dma(q3[(2 * tc) % 3], xst[:, sl, :], ch,
                       reads=(st_["xst"][sl], st_["x_load_extra"][tc]))
            rb = (tc + 2) % 3
            mm = None
            for jb in range(2):
                b.op("tensor", lambda E, j2=jb, r=rb: E.matmul(
                    ops_[r][:, j2 * 512:(j2 + 1) * 512], onesrow[0:1, :],
                    bo_b[0:1, j2 * 512:(j2 + 1) * 512], start=True, stop=False),
                    twt + (at_done, ops_tok[rb], t_bo, t_or) if jb == 0 else (),
                    None)
                for c in range(8):
                    mm = b.op("tensor", lambda E, c3=c, j2=jb, t2=tc, r=rb, w=wTs:
                              E.matmul(ops_[r][:, j2 * 512:(j2 + 1) * 512],
                                       at[:, c3, t2 * 128:(t2 + 1) * 128],
                                       w[:, c3, j2 * 512:(j2 + 1) * 512],
                                       start=False, stop=(c3 == 7)),
                              (), "pe" if (jb == 1 and c == 7) else None)
            tadd = b.op("vector", lambda E, s=sl, r=rb, xo=xout[sl]:
                        E.tensor_tensor(
                            out=xo, in0=ops_[r][:, :], in1=xst[:, s, :],
                            op=ALU.add), (mm, tx, xout_tok[sl]), "dv")
            ops_tok[rb] = tadd
            st_["xst"][sl] = tadd
            tstore = b.dma(q3[(2 * tc + 1) % 3], ch, xout[sl], reads=(tadd,))
            xout_tok[sl] = tstore
            st_["prev_store"] = tstore
            st_["x_load_extra"][tc] = tstore
        st_["pb"][0] = ops_tok[0]
        st_["pb"][1] = ops_tok[0]
        st_["pb"][2] = ops_tok[1]
        st_["pb"][3] = ops_tok[1]
        st_["pa"][0] = ops_tok[2]
        st_["pa"][1] = ops_tok[2]
        stackE.close()
        stackP.close()
        st_["prev_l2_pe"] = st_["e2_pe"]

    st_["ln2_pre"] = [None] * 16

    def ln2_hook(gc, page, gate):
        if page is None:
            sl = gc % 2
            tx = b.dma("scalar", xst[:, sl, :], chunk_ap(x_dram, gc),
                       reads=(st_["xst"][sl], st_["x_load_extra"][gc]))
            pg = xst[:, sl, :]
            r = (tx,)
        else:
            pg = page
            r = gate
        b1 = b.op("vector", lambda E, p=pg: E.bn_stats(
            out=stat[:, 0, 0:6], in_=p[:, 0:512]), r)
        b2 = b.op("vector", lambda E, p=pg: E.bn_stats(
            out=stat[:, 0, 6:12], in_=p[:, 512:1024]), r)
        mrow = 1 if gc < 8 else 6
        mv = stat[:, mrow, 2 * (gc % 8):2 * (gc % 8) + 2]
        ag = b.op("vector", lambda E, m=mv: E.bn_aggr(
            out=m, in_=stat[:, 0, 0:12]), (b1, b2))
        tln = b.op("scalar", lambda E, m=mv, cc=gc: E.activation(
            out=stat[:, 7, cc:cc + 1], in_=m[:, 1:2],
            func=AF.Ln, bias=epsc[:, 0:1]), (ag, t_ep))
        trs = b.op("scalar", lambda E, cc=gc: E.activation(
            out=stat[:, 3, cc:cc + 1], in_=stat[:, 7, cc:cc + 1],
            func=AF.Exp, scale=-0.5), (tln,))
        if page is None:
            st_["xst"][gc % 2] = (b1, b2)
        st_["ln2_pre"][gc] = (mv, trs, (b1, b2))

    n_ph = len(debug_phases)
    assert "ffn1" in debug_phases
    if n_ph > 1:
        st_["prev_xh_store"] = tuple(
            [(ds, b.cnt[ds], "_") for ds in b.dma_sems if b.cnt[ds]]
            + [("ac", b.cnt["ac"], "_")])
        st_["wprep_fn"] = wprep
    with ExitStack() as fstack:
        ffn_phase(x_in, x_dram if n_ph > 1 else out_d, fstack,
                  l2_hook=ln2_hook if "attn" in debug_phases else None)
    st_["ln2_stage_gate"] = ("dv", b.cnt["dv"], "_")
    if "attn" in debug_phases:
        b.fence()
        with ExitStack() as astack:
            attn_phase(astack)
    if "ffn2" in debug_phases:
        b.phase('ffn2')
        b.fence()
        with ExitStack() as fstack:
            ffn_phase(x_dram, out_d, fstack)
    elif n_ph > 1:
        for gc in range(16):
            sl = gc % 2
            tx = b.dma("sync", xst[:, sl, :], chunk_ap(x_dram, gc),
                       reads=(st_["xst"][sl], st_["prev_store"]))
            tst = b.dma("sync", chunk_ap(out_d, gc), xst[:, sl, :], reads=(tx,))
            st_["xst"][sl] = tst

    for ds in b.dma_sems:
        if b.cnt[ds]:
            b.streams["sync"].append(("w", ds, b.cnt[ds]))
    b.replay()
    return b


_PHASES = ("ffn1", "attn", "ffn2")


def kernel(L0_attn_x, L0_attn_Wq, L0_attn_Wk, L0_attn_Wv, L0_attn_Wo,
           L0_attn_bo, L0_ffn_k0, L0_ffn_k1, L0_ffn_k2,
           L0_ffn_s0, L0_ffn_s1, L0_ffn_s2):
    x = np.ascontiguousarray(np.asarray(L0_attn_x, dtype=np.float32))
    k0 = np.ascontiguousarray(np.asarray(L0_ffn_k0)).view(np.uint8)
    k1 = np.ascontiguousarray(np.asarray(L0_ffn_k1)).view(np.uint8)
    k2 = np.ascontiguousarray(np.asarray(L0_ffn_k2)).view(np.uint8)
    ws = {w: np.ascontiguousarray(np.asarray(a, dtype=np.float32))
          for w, a in zip("qkvo", (L0_attn_Wq, L0_attn_Wk, L0_attn_Wv, L0_attn_Wo))}
    bo = np.asarray(L0_attn_bo, dtype=np.float32).reshape(1, D)
    sv = np.ascontiguousarray(np.tile(
        np.stack([np.asarray(L0_ffn_s0, dtype=np.float32),
                  np.asarray(L0_ffn_s1, dtype=np.float32),
                  np.asarray(L0_ffn_s2, dtype=np.float32)]).reshape(1, 3),
        (128, 1)))
    ident = np.eye(128, dtype=np.float32).astype(ml_dtypes.bfloat16)

    b = build_program(_PHASES)
    in_maps = []
    for i in range(NCORES):
        in_maps.append(dict(
            x_in=x[i], k0=k0, k1=k1, k2=k2,
            Wq=ws["q"], Wk=ws["k"], Wv=ws["v"], Wo=ws["o"],
            bo=bo, sv=sv, ident=ident))
    res = run_bass_kernel_spmd(b.nc, in_maps, core_ids=list(range(NCORES)))
    return np.stack([r["out"] for r in res.results], axis=0)
